# version 04 (final)
# baseline (speedup 1.0000x reference)
import numpy as np
import concourse.bass as bass
import concourse.mybir as mybir
from concourse.bass_utils import run_bass_kernel_spmd

F32 = mybir.dt.float32
BF16 = mybir.dt.bfloat16
AF = mybir.ActivationFunctionType
ALU = mybir.AluOpType

T = 4096
D = 1024
TC = 512
NTC = T // TC
KC = 64
NCH = T // KC
P = 128

ENGS = ("sync", "tensor", "vector", "scalar", "gpsimd")
STOP = 99
START = 0
E88_SKIP_A = False
DSS_NO_CC = False
DSS_NO_K = False
DSS_NO_TAPS = False
DSS_TAIL_ONLY = False
TWO_PI = 6.283185307179586
PI = 3.141592653589793

(C_ONE, C_2PI, C_PI, C_PI2, C_EPS5, C_INVD, C_INV512, C_NEG1, C_E12, C_NEGH,
 C_N64P, C_N32P, C_N16P, C_N8P, C_N4P, C_N2P, C_P2J, C_N6TH, C_1_24,
 C_1_120, C_N5040, C_N720, C_1_40320, C_EIGHT, C_HALF, C_INV4K,
 C_EPS6, C_INV64) = range(28)
N_CCOLS = 28
SINCOS_J = 8
NEG_PI_MULT = [(C_N64P, 64), (C_N32P, 32), (C_N16P, 16), (C_N8P, 8),
               (C_N4P, 4), (C_N2P, 2)]


class Build:
    def __init__(self, nc):
        self.nc = nc
        self.q = {e: [] for e in ENGS}
        self.sems = {}
        self.cnt = {}

    def sem(self, name):
        if name not in self.sems:
            self.sems[name] = self.nc.semaphore(name).__enter__()
            self.cnt[name] = 0
        return self.sems[name]

    ENG_SEM = {"vector": "vv", "scalar": "ss", "tensor": "mm", "gpsimd": "gg"}

    def op(self, eng, fn, inc=None, n=1, waits=(), chain=False):
        if chain:
            es = self.ENG_SEM[eng]
            self.sem(es)
            waits = tuple(waits) + ((es, self.cnt[es]),)
            if inc is None:
                inc = es
        waits = tuple((s, v) for (s, v) in waits if v > 0)
        for s, v in waits:
            self.sem(s)
            assert v <= self.cnt[s], (s, v, self.cnt[s], "wait on future event")
        ret = None
        if inc is not None:
            self.sem(inc)
            self.cnt[inc] += n
            ret = self.cnt[inc]
        sems = self.sems
        inc_h = sems[inc] if inc is not None else None

        def run(h, fn=fn, waits=waits, inc_h=inc_h, n=n):
            for s, v in waits:
                h.wait_ge(sems[s], v)
            inst = fn(h)
            if inc_h is not None:
                if inst is None:
                    h.nop().then_inc(inc_h, n)
                else:
                    inst.then_inc(inc_h, n)
        self.q[eng].append(run)
        return ret

    def vop(self, fn, inc=None, n=1, waits=(), chain=True):
        return self.op("vector", fn, inc=inc, n=n, waits=waits, chain=chain)

    def sop(self, fn, inc=None, n=1, waits=(), chain=True):
        return self.op("scalar", fn, inc=inc, n=n, waits=waits, chain=chain)

    def dma(self, eng, out, in_, inc="dL", waits=()):
        return self.op(eng, lambda h: h.dma_start(out=out, in_=in_),
                       inc=inc, n=16, waits=waits)

    def val(self, sem):
        self.sem(sem)
        return self.cnt[sem]

    def barrier(self):
        s = "bar"
        self.sem(s)
        self.sem("dL")
        self.sem("dG")
        for e in ENGS:
            self.cnt[s] += 1
        tgt = self.cnt[s]
        dlv = self.cnt["dL"]
        dgv = self.cnt["dG"]
        sems = self.sems

        def run_sync(h, tgt=tgt, dlv=dlv):
            if dlv:
                h.wait_ge(sems["dL"], dlv)
            h.nop().then_inc(sems[s], 1)
            h.wait_ge(sems[s], tgt)

        def run_gp(h, tgt=tgt, dgv=dgv):
            if dgv:
                h.wait_ge(sems["dG"], dgv)
            h.nop().then_inc(sems[s], 1)
            h.wait_ge(sems[s], tgt)

        def run(h, tgt=tgt):
            h.nop().then_inc(sems[s], 1)
            h.wait_ge(sems[s], tgt)
        for e in ENGS:
            self.q[e].append(run_sync if e == "sync" else (run_gp if e == "gpsimd" else run))

    def finalize(self, block):
        for e in ENGS:
            ops = list(self.q[e])

            def body(h, ops=ops):
                for f in ops:
                    f(h)
            getattr(block, e)(body)


class Ctx:
    def __init__(self, nc, dbg=None):
        self.nc = nc
        self.b = Build(nc)
        self.io = {}
        self.sb = {}
        self._scopes = [[]]
        self.ps = []
        self._ps_cond = []
        self._ps_inflight = [False] * 8
        self._ps_rr = 0
        self.dbg = dbg

    def push(self):
        self._scopes.append([])

    def pop(self):
        self.b.barrier()
        for h in reversed(self._scopes.pop()):
            h.__exit__(None, None, None)

    def sbuf(self, name, shape, dtype):
        self._name_ctr = getattr(self, "_name_ctr", 0) + 1
        h = self.nc.sbuf_tensor(f"{name}_{self._name_ctr}", list(shape), dtype)
        t = h.__enter__()
        self._scopes[-1].append(h)
        self.sb[name] = t
        return t

    def psum_init(self):
        for i in range(8):
            h = self.nc.psum_tensor(f"pb{i}", [P, TC], F32)
            t = h.__enter__()
            self._scopes[0].append(h)
            self.ps.append(t)
            self._ps_cond.append(None)

    def bank(self):
        for _ in range(8):
            i = self._ps_rr % 8
            self._ps_rr += 1
            if not self._ps_inflight[i]:
                cond = self._ps_cond[i]
                self._ps_inflight[i] = True
                self._ps_cond[i] = None
                return i, cond
        raise AssertionError("all PSUM banks in flight")

    def bank_release(self, i, sem, val):
        self._ps_cond[i] = (sem, val)
        self._ps_inflight[i] = False

    def wc(self, *conds):
        return [cd for cd in conds if cd is not None]



def host_prep(inputs):
    g = {k: np.ascontiguousarray(np.asarray(v)) for k, v in inputs.items()}
    dev = {}
    t = lambda a: np.ascontiguousarray(np.asarray(a).T)
    col = lambda a: np.ascontiguousarray(np.asarray(a).reshape(-1, 1))
    row = lambda a: np.ascontiguousarray(np.asarray(a).reshape(1, -1))

    cc = np.zeros((P, N_CCOLS), dtype=np.float32)
    cc[:, C_ONE] = 1.0
    cc[:, C_2PI] = TWO_PI
    cc[:, C_PI] = PI
    cc[:, C_PI2] = PI / 2
    cc[:, C_EPS5] = 1e-5
    cc[:, C_INVD] = 1.0 / D
    cc[:, C_INV512] = 1.0 / 512
    cc[:, C_NEG1] = -1.0
    cc[:, C_E12] = 1e12
    cc[:, C_NEGH] = -0.5
    for idx, mult in NEG_PI_MULT:
        cc[:, idx] = -PI * mult
    cc[:, C_P2J] = 2.0 ** (-SINCOS_J)
    cc[:, C_N6TH] = -1.0 / 6.0
    cc[:, C_1_24] = 1.0 / 24.0
    cc[:, C_1_120] = 1.0 / 120.0
    cc[:, C_N5040] = -1.0 / 5040.0
    cc[:, C_N720] = -1.0 / 720.0
    cc[:, C_1_40320] = 1.0 / 40320.0
    cc[:, C_EIGHT] = 8.0
    cc[:, C_HALF] = 0.5
    cc[:, C_INV4K] = 1.0 / 4096
    cc[:, C_EPS6] = 1e-6
    cc[:, C_INV64] = 1.0 / 64
    dev["c_cols"] = cc
    dev["c_iota"] = np.broadcast_to(
        np.arange(KC + 1, dtype=np.float32), (P, KC + 1)).copy()
    dev["c_riota"] = np.broadcast_to(
        np.arange(KC - 1, -1, -1, dtype=np.float32), (P, KC)).copy()
    dev["c_ones_col"] = np.ones((P, 1), dtype=np.float32)
    dev["c_ones_row"] = np.ones((1, TC), dtype=np.float32)
    dev["c_eye"] = np.eye(P, dtype=np.float32)

    dev["aa_n1_w"] = col(g["L0_attn_a_n1_w"])
    dev["aa_n1_b"] = col(g["L0_attn_a_n1_b"])
    dev["aa_Wxg"] = g["L0_attn_a_Wxg"]
    dev["aa_Wdin"] = g["L0_attn_a_Wdin"]
    dev["aa_n2_w"] = col(g["L0_attn_a_n2_w"])
    dev["aa_n2_b"] = col(g["L0_attn_a_n2_b"])
    dev["aa_LR"] = col(g["L0_attn_a_LR"])
    dev["aa_LI"] = col(g["L0_attn_a_LI"])
    dev["aa_CRt"] = t(g["L0_attn_a_CR"])
    dev["aa_CIt"] = t(g["L0_attn_a_CI"])
    dev["aa_D"] = col(g["L0_attn_a_D"])
    dev["aa_Wgate"] = g["L0_attn_a_Wgate"]
    dev["aa_Wout"] = g["L0_attn_a_Wout"]
    dev["aa_bout"] = col(g["L0_attn_a_bout"])

    HID = 2730
    HIDP = 2816
    wqkv = g["L0_ffn_wqkv"]
    wo = g["L0_ffn_wo"]
    w1 = g["L0_ffn_w1"]
    b1 = g["L0_ffn_b1"]
    w2 = g["L0_ffn_w2"]
    for l in range(2):
        dev[f"ff_wqkv{l}"] = wqkv[l]
        dev[f"ff_wo{l}"] = wo[l]
        w1a = np.zeros((1024, HIDP), np.float32)
        w1g = np.zeros((1024, HIDP), np.float32)
        w1a[:, :HID] = w1[l][:, :HID]
        w1g[:, :HID] = w1[l][:, HID:]
        dev[f"ff_w1a{l}"] = w1a
        dev[f"ff_w1g{l}"] = w1g
        b1a = np.zeros((HIDP, 1), np.float32)
        b1g = np.zeros((HIDP, 1), np.float32)
        b1a[:HID, 0] = b1[l][:HID]
        b1g[:HID, 0] = b1[l][HID:]
        dev[f"ff_b1a{l}"] = b1a
        dev[f"ff_b1g{l}"] = b1g
        w2p = np.zeros((HIDP, 1024), np.float32)
        w2p[:HID] = w2[l]
        dev[f"ff_w2{l}"] = w2p
        dev[f"ff_b2{l}"] = col(g["L0_ffn_b2"][l])
        dev[f"ff_lng{l}"] = col(g["L0_ffn_ln_g"][l])
        dev[f"ff_lng{l}r"] = row(g["L0_ffn_ln_g"][l])
        dev[f"ff_lnb{l}r"] = row(g["L0_ffn_ln_b"][l])
        dev[f"ff_lnb{l}"] = col(g["L0_ffn_ln_b"][l])
        dev[f"ff_qs{l}"] = np.ascontiguousarray(
            np.tile(g["L0_ffn_q_scale"][l], 2).reshape(128, 1))
        dev[f"ff_ks{l}"] = np.ascontiguousarray(
            np.tile(g["L0_ffn_k_scale"][l], 2).reshape(128, 1))
    DH = 64
    WINDOW = 512
    half = np.arange(0, DH, 2, dtype=np.float64)
    inv_freq = 1.0 / 10000.0 ** (half / DH)
    tpos = np.arange(T, dtype=np.float64)
    freqs = tpos[:, None] * inv_freq[None, :]
    freqs2 = np.concatenate([freqs, freqs], axis=1)
    base = (half + 0.4 * DH) / (1.4 * DH)
    power = (tpos - T // 2) / float(WINDOW // 2)
    scl = base[None, :] ** power[:, None]
    scl2 = np.concatenate([scl, scl], axis=1)
    cos_ = np.cos(freqs2)
    sin_ = np.sin(freqs2)
    sgn = np.ones(64)
    sgn[:32] = -1.0
    cq = (cos_ * scl2).T
    sq = (sin_ * scl2 * sgn[None, :].T.ravel()[:, None].T).T
    ck = (cos_ / scl2).T
    sk = (sin_ / scl2 * sgn[None, :]).T
    dev["c_rotcq"] = np.ascontiguousarray(np.tile(cq, (2, 1)).astype(np.float32))
    dev["c_rotsq"] = np.ascontiguousarray(np.tile(sq, (2, 1)).astype(np.float32))
    dev["c_rotck"] = np.ascontiguousarray(np.tile(ck, (2, 1)).astype(np.float32))
    dev["c_rotsk"] = np.ascontiguousarray(np.tile(sk, (2, 1)).astype(np.float32))
    blk = np.zeros((P, 2), np.float32)
    blk[:64, 0] = 1.0
    blk[64:, 1] = 1.0
    dev["c_blk64"] = blk
    dev["c_blk64T"] = np.ascontiguousarray(blk.T)
    Pm = np.zeros((P, P), np.float32)
    for hh in range(2):
        for i in range(32):
            Pm[hh * 64 + 32 + i, hh * 64 + i] = 1.0
            Pm[hh * 64 + i, hh * 64 + 32 + i] = 1.0
    dev["c_swap"] = Pm
    masks = np.zeros((8, 128, 512), np.float32)
    for kc in range(8):
        j = np.arange(128)[:, None]
        i = np.arange(512)[None, :]
        qmk = 512 + i - 128 * kc - j
        if kc < 4:
            valid = (qmk >= 0) & (qmk <= WINDOW)
            valid &= (qmk <= WINDOW)
        else:
            qmk = i - (128 * (kc - 4) + j)
            valid = (qmk >= 0) & (qmk <= WINDOW)
        masks[kc] = valid.astype(np.float32)
    dev["c_lamask"] = np.ascontiguousarray(masks.reshape(8 * 128, 512))

    dev["ab_WQt"] = t(g["L0_attn_b_WQ"])
    dev["ab_WKt"] = t(g["L0_attn_b_WK"])
    dev["ab_WVt"] = t(g["L0_attn_b_WV"])
    dev["ab_WOt"] = t(g["L0_attn_b_WO"])
    dev["ab_Wgt"] = t(g["L0_attn_b_Wg"])
    dev["ab_bQ"] = col(g["L0_attn_b_bQ"])
    dev["ab_bK"] = col(g["L0_attn_b_bK"])
    dev["ab_bVr"] = row(g["L0_attn_b_bV"])
    dev["ab_bKr"] = row(g["L0_attn_b_bK"])
    dev["ab_bO"] = col(g["L0_attn_b_bO"])
    dev["ab_bg"] = col(g["L0_attn_b_bg"])
    dev["ab_alog"] = np.ascontiguousarray(
        g["L0_attn_b_alpha_logit"].reshape(16 * 64, 1))
    dev["ab_lgam"] = col(g["L0_attn_b_ln_gamma"])
    dev["ab_lbet"] = col(g["L0_attn_b_ln_beta"])
    dev["c_i64t"] = np.broadcast_to(
        np.tile(np.arange(64, dtype=np.float32), 8), (P, 512)).copy()
    dev["c_ri64"] = np.broadcast_to(
        np.arange(63, -1, -1, dtype=np.float32), (P, 64)).copy()
    ut = np.triu(np.ones((64, 64), np.float32))
    dev["c_ut64"] = np.ascontiguousarray(np.tile(ut, (1, 8)))

    dev["eb_WqT"] = t(g["L1_attn_b_Wq"])
    dev["eb_WkT"] = t(g["L1_attn_b_Wk"])
    dev["eb_WvT"] = t(g["L1_attn_b_Wv"])
    dev["eb_WgT"] = t(g["L1_attn_b_Wg"])
    dev["eb_WaT"] = t(g["L1_attn_b_Wa"])
    dev["eb_WoT"] = t(g["L1_attn_b_Wo"])
    dev["eb_qcw"] = np.ascontiguousarray(g["L1_attn_b_q_conv_w"].reshape(256, 4))
    dev["eb_kcw"] = np.ascontiguousarray(g["L1_attn_b_k_conv_w"].reshape(256, 4))
    dev["eb_vcw"] = np.ascontiguousarray(g["L1_attn_b_v_conv_w"].reshape(512, 4))
    dev["eb_alog"] = col(g["L1_attn_b_A_log"])
    dev["eb_dtb"] = col(g["L1_attn_b_dt_bias"])
    dev["eb_onw"] = col(np.tile(g["L1_attn_b_o_norm_w"], 8))
    bd4 = np.zeros((4, 128), np.float32)
    for hh in range(4):
        bd4[hh, hh * 32:(hh + 1) * 32] = 1.0
    dev["c_bd4T"] = bd4
    bo32 = np.zeros((128, 4), np.float32)
    for j in range(4):
        bo32[j * 32:(j + 1) * 32, j] = 1.0
    dev["c_bo32"] = bo32
    bo64 = np.zeros((128, 2), np.float32)
    bo64[0:64, 0] = 1.0
    bo64[64:128, 1] = 1.0
    dev["c_bo64"] = bo64
    rep2 = np.zeros((2, 128), np.float32)
    rep2[0, 0:64] = 1.0
    rep2[1, 64:128] = 1.0
    dev["c_rep2"] = rep2

    dev["fb_w1T"] = t(g["L1_ffn_w1"])
    dev["fb_w2T"] = t(g["L1_ffn_w2"])
    dev["fb_b1"] = col(g["L1_ffn_b1"])
    dev["fb_b2"] = col(g["L1_ffn_b2"])

    dev["ma_Win"] = np.ascontiguousarray(g["L1_attn_a_W_in"])
    dev["ma_Wdelta"] = np.ascontiguousarray(g["L1_attn_a_W_delta"])
    dev["ma_bdelta"] = col(g["L1_attn_a_b_delta"])
    dev["ma_WB"] = np.ascontiguousarray(g["L1_attn_a_W_B"])
    dev["ma_WC"] = np.ascontiguousarray(g["L1_attn_a_W_C"])
    dev["ma_Alog"] = np.ascontiguousarray(g["L1_attn_a_A_log"])
    dev["ma_D"] = col(g["L1_attn_a_D_param"])
    dev["ma_Wout"] = np.ascontiguousarray(g["L1_attn_a_W_out"])

    x = g["L0_attn_a_idx"]
    per_core = [{"x_in": t(x[c // 2])} for c in range(8)]
    return dev, per_core


INPUT_SPECS = [
    ("x_in", [D, T]),
    ("c_cols", [P, N_CCOLS]), ("c_iota", [P, KC + 1]), ("c_riota", [P, KC]),
    ("c_ones_col", [P, 1]), ("c_ones_row", [1, TC]), ("c_eye", [P, P]),
    ("aa_n1_w", [D, 1]), ("aa_n1_b", [D, 1]),
    ("aa_Wxg", [D, 4096]), ("aa_Wdin", [D, 512]),
    ("aa_n2_w", [512, 1]), ("aa_n2_b", [512, 1]),
    ("aa_LR", [512, 1]), ("aa_LI", [512, 1]),
    ("aa_CRt", [512, 512]), ("aa_CIt", [512, 512]),
    ("aa_D", [512, 1]),
    ("aa_Wgate", [512, 4096]), ("aa_Wout", [4096, D]),
    ("aa_bout", [D, 1]),
    ("ff_wqkv0", [D, 1536]), ("ff_wqkv1", [D, 1536]),
    ("ff_wo0", [512, D]), ("ff_wo1", [512, D]),
    ("ff_w1a0", [D, 2816]), ("ff_w1g0", [D, 2816]),
    ("ff_w1a1", [D, 2816]), ("ff_w1g1", [D, 2816]),
    ("ff_b1a0", [2816, 1]), ("ff_b1g0", [2816, 1]),
    ("ff_b1a1", [2816, 1]), ("ff_b1g1", [2816, 1]),
    ("ff_w20", [2816, D]), ("ff_w21", [2816, D]),
    ("ff_b20", [D, 1]), ("ff_b21", [D, 1]),
    ("ff_lng0", [D, 1]), ("ff_lnb0", [D, 1]),
    ("ff_lng1", [D, 1]), ("ff_lnb1", [D, 1]),
    ("ff_lng0r", [1, D]), ("ff_lnb0r", [1, D]),
    ("ff_lng1r", [1, D]), ("ff_lnb1r", [1, D]),
    ("ff_qs0", [P, 1]), ("ff_ks0", [P, 1]),
    ("ff_qs1", [P, 1]), ("ff_ks1", [P, 1]),
    ("c_rotcq", [P, T]), ("c_rotsq", [P, T]),
    ("c_rotck", [P, T]), ("c_rotsk", [P, T]),
    ("c_blk64", [P, 2]), ("c_blk64T", [2, P]), ("c_swap", [P, P]),
    ("c_lamask", [8 * P, 512]),
    ("ab_WQt", [D, D]), ("ab_WKt", [D, D]), ("ab_WVt", [D, D]),
    ("ab_WOt", [D, D]), ("ab_Wgt", [D, D]),
    ("ab_bQ", [D, 1]), ("ab_bK", [D, 1]), ("ab_bVr", [1, D]), ("ab_bKr", [1, D]),
    ("ab_bO", [D, 1]), ("ab_bg", [D, 1]),
    ("ab_alog", [16 * 64, 1]), ("ab_lgam", [D, 1]), ("ab_lbet", [D, 1]),
    ("c_i64t", [P, 512]), ("c_ri64", [P, 64]), ("c_ut64", [64, 512]),
    ("ma_Win", [D, 4096]), ("ma_Wdelta", [2048, 2048]), ("ma_bdelta", [2048, 1]),
    ("ma_WB", [2048, 16]), ("ma_WC", [2048, 16]), ("ma_Alog", [2048, 16]),
    ("ma_D", [2048, 1]), ("ma_Wout", [2048, D]),
    ("fb_w1T", [D, 4096]), ("fb_w2T", [4096, D]),
    ("fb_b1", [4096, 1]), ("fb_b2", [D, 1]),
    ("eb_WqT", [D, 256]), ("eb_WkT", [D, 256]), ("eb_WvT", [D, 512]),
    ("eb_WgT", [D, 512]), ("eb_WaT", [D, 8]), ("eb_WoT", [512, D]),
    ("eb_qcw", [256, 4]), ("eb_kcw", [256, 4]), ("eb_vcw", [512, 4]),
    ("eb_alog", [8, 1]), ("eb_dtb", [8, 1]), ("eb_onw", [512, 1]),
    ("c_bd4T", [4, 128]), ("c_bo32", [128, 4]), ("c_bo64", [128, 2]),
    ("c_rep2", [2, 128]),
]


def alloc_io(c, dbg_shape):
    nc, io = c.nc, c.io
    for name, shape in INPUT_SPECS:
        io[name] = nc.dram_tensor(name, list(shape), F32, kind="ExternalInput").ap()
    io["xres"] = nc.dram_tensor("xres", [D, T], F32, kind="Internal").ap()
    io["xn_hbm"] = nc.dram_tensor("xn_hbm", [D, T], BF16, kind="Internal").ap()
    io["xg_st"] = nc.dram_tensor("xg_st", [4096, T], BF16, kind="Internal").ap()
    io["gxg_st"] = nc.dram_tensor("gxg_st", [4096, T], BF16, kind="Internal").ap()
    io["ydss_st"] = nc.dram_tensor("ydss_st", [512, T], BF16, kind="Internal").ap()
    io["xi_st"] = nc.dram_tensor("xi_st", [D, T], F32, kind="Internal").ap()
    io["hp_st"] = nc.dram_tensor("hp_st", [2816, T], BF16, kind="Internal").ap()
    io["rows1_st"] = nc.dram_tensor("rows1_st", [2, T], F32, kind="Internal").ap()
    io["ur_st"] = nc.dram_tensor("ur_st", [4096, T], BF16, kind="Internal").ap()
    io["dl_st"] = nc.dram_tensor("dl_st", [2048, T], F32, kind="Internal").ap()
    io["yg_st"] = nc.dram_tensor("yg_st", [2048, T], BF16, kind="Internal").ap()
    io["Bc_st"] = nc.dram_tensor("Bc_st", [16, T], F32, kind="Internal").ap()
    io["Cc_st"] = nc.dram_tensor("Cc_st", [16, T], F32, kind="Internal").ap()
    io["sg1_st"] = nc.dram_tensor("sg1_st", [D, 4096], BF16, kind="Internal").ap()
    io["sg2_st"] = nc.dram_tensor("sg2_st", [4096, D], BF16, kind="Internal").ap()
    io["al_st"] = nc.dram_tensor("al_st", [2, 4096], F32, kind="Internal").ap()
    io["kn_st"] = nc.dram_tensor("kn_st", [256, T], F32, kind="Internal").ap()
    io["qn_st"] = nc.dram_tensor("qn_st", [256, T], F32, kind="Internal").ap()
    io["v_st"] = nc.dram_tensor("v_st", [512, T], BF16, kind="Internal").ap()
    io["gt_st"] = nc.dram_tensor("gt_st", [512, T], BF16, kind="Internal").ap()
    io["dec_st"] = nc.dram_tensor("dec_st", [8, T], F32, kind="Internal").ap()
    io["rows3_st"] = nc.dram_tensor("rows3_st", [2, T], F32, kind="Internal").ap()
    io["out"] = nc.dram_tensor("out", [D, T], F32, kind="ExternalOutput").ap()
    if dbg_shape is not None:
        io["dbg"] = nc.dram_tensor("dbg", list(dbg_shape), F32,
                                   kind="ExternalOutput").ap()


def cc1(c, col):
    return c.sb["ccols"][0:1, col:col + 1]



def ln_stats_from_tiles(c, bf_tiles, sq_tiles):
    b = c.b
    onesb = c.sb["ones_bf"]
    i0, cd0 = c.bank()
    i1, cd1 = c.bank()
    pa, pbq = c.ps[i0], c.ps[i1]
    n = len(bf_tiles)
    mm_guard = b.val("mm")
    for k, ap in enumerate(bf_tiles):
        b.op("tensor", lambda h, ap=ap, k=k, n=n: h.matmul(
            out=pa[0:1, 0:TC], lhsT=onesb[:], rhs=ap,
            start=(k == 0), stop=(k == n - 1)),
            inc="mm" if k == n - 1 else None, waits=c.wc(cd0) if k == 0 else ())
    for k, ap in enumerate(sq_tiles):
        b.op("tensor", lambda h, ap=ap, k=k, n=n: h.matmul(
            out=pbq[0:1, 0:TC], lhsT=onesb[:], rhs=ap,
            start=(k == 0), stop=(k == n - 1)),
            inc="mm" if k == n - 1 else None, waits=c.wc(cd1) if k == 0 else ())
    mmv = b.val("mm")
    b.op("vector", lambda h: h.tensor_copy(out=c.sb["row_s0"][0:1, 0:TC],
                                           in_=pa[0:1, 0:TC]),
         inc="vv", waits=[("mm", mmv)])
    vv = b.op("vector", lambda h: h.tensor_copy(out=c.sb["row_s1"][0:1, 0:TC],
                                                in_=pbq[0:1, 0:TC]), inc="vv")
    c.bank_release(i0, "vv", vv)
    c.bank_release(i1, "vv", vv)
    return vv


def ln_rows_math(c, inv_col, fused_n1=False):
    b = c.b
    s0 = c.sb["row_s0"][0:1, 0:TC]
    s1 = c.sb["row_s1"][0:1, 0:TC]
    m_ = c.sb["row_m"][0:1, 0:TC]
    rr = c.sb["rowr"][0:1, 0:TC]
    mr = c.sb["rowmr"][0:1, 0:TC]
    sc = c.sb["row_sc"][0:1, 0:TC]
    inv = cc1(c, inv_col)
    b.op("scalar", lambda h: h.activation(out=m_, in_=s0, func=AF.Copy, scale=inv),
         inc="ss", waits=[("vv", b.val("vv"))])
    b.op("scalar", lambda h: h.activation(out=s1, in_=s1, func=AF.Copy, scale=inv),
         inc="ss")
    sv = b.val("ss")
    b.op("vector", lambda h: h.tensor_mul(out=sc, in0=m_, in1=m_), inc="vv",
         waits=[("ss", sv)])
    b.op("vector", lambda h: h.tensor_sub(out=s1, in0=s1, in1=sc), inc="vv")
    vv = b.val("vv")
    b.op("scalar", lambda h: h.activation(out=rr, in_=s1, func=AF.Ln,
                                          bias=cc1(c, C_EPS5)), inc="ss",
         waits=[("vv", vv)])
    if fused_n1:
        b.op("scalar", lambda h: h.activation(out=sc, in_=rr, func=AF.Exp,
                                              scale=cc1(c, C_NEG1)), inc="ss")
        b.op("vector", lambda h: h.tensor_mul(out=sc, in0=s1, in1=sc), inc="vv",
             waits=[("ss", b.val("ss"))])
        b.op("scalar", lambda h: h.activation(out=sc, in_=sc, func=AF.Ln,
                                              bias=cc1(c, C_EPS5)), inc="ss",
             waits=[("vv", b.val("vv"))])
        b.op("vector", lambda h: h.tensor_add(out=rr, in0=rr, in1=sc), inc="vv",
             waits=[("ss", b.val("ss"))])
        b.op("scalar", lambda h: h.activation(out=rr, in_=rr, func=AF.Exp,
                                              scale=cc1(c, C_NEGH)), inc="ss",
             waits=[("vv", b.val("vv"))])
    else:
        b.op("scalar", lambda h: h.activation(out=rr, in_=rr, func=AF.Exp,
                                              scale=cc1(c, C_NEGH)), inc="ss")
    b.op("vector", lambda h: h.tensor_mul(out=mr, in0=m_, in1=rr), inc="vv",
         waits=[("ss", b.val("ss"))])
    vv = b.op("vector", lambda h: h.tensor_single_scalar(
        out=mr, in_=mr, scalar=cc1(c, C_NEG1), op=ALU.mult), inc="vv")
    return vv


def rows_from_mv(c, src_hbm, t0):
    b = c.b
    m_ = c.sb["row_m"][0:1, 0:TC]
    s1 = c.sb["row_s1"][0:1, 0:TC]
    rr = c.sb["rowr"][0:1, 0:TC]
    mr = c.sb["rowmr"][0:1, 0:TC]
    b.dma("sync", m_, src_hbm[0:1, t0:t0 + TC])
    b.dma("sync", s1, src_hbm[1:2, t0:t0 + TC])
    dvr = b.val("dL")
    b.sop(lambda h: h.activation(out=rr, in_=s1, func=AF.Ln,
                                 bias=cc1(c, C_EPS5)), waits=[("dL", dvr)])
    b.sop(lambda h: h.activation(out=rr, in_=rr, func=AF.Exp,
                                 scale=cc1(c, C_NEGH)))
    b.vop(lambda h: h.tensor_mul(out=mr, in0=m_, in1=rr),
          waits=[("ss", b.val("ss"))])
    b.vop(lambda h: h.tensor_single_scalar(out=mr, in_=mr,
                                           scalar=cc1(c, C_NEG1), op=ALU.mult))
    return b.val("vv")


def bcast_row(c, row_ap, waits):
    b = c.b
    i, cond = c.bank()
    pt = c.ps[i]
    onesr = c.sb["ones_row"]
    mmv = b.op("tensor", lambda h: h.matmul(
        out=pt[:, 0:TC], lhsT=onesr[0:1, 0:P], rhs=row_ap, start=True, stop=True),
        inc="mm", waits=list(waits) + c.wc(cond))
    return i, mmv


def wload(c, src, r0, nrt, col0, ncols, wf, wb):
    b = c.b
    for k in range(nrt):
        b.dma("sync", wf[:, k * ncols:(k + 1) * ncols],
              src[r0 + k * P:r0 + (k + 1) * P, col0:col0 + ncols])
    dv = b.val("dL")
    vv = None
    for k in range(nrt):
        vv = b.op("vector", lambda h, k=k: h.tensor_copy(
            out=wb[:, k * ncols:(k + 1) * ncols], in_=wf[:, k * ncols:(k + 1) * ncols]),
            inc="vv", waits=[("dL", dv)] if k == 0 else ())
    return vv


def dbg_tap_bf(c, tag, src_fn, ntiles, width):
    if c.dbg != tag:
        return
    b = c.b
    tmp = c.sb["dbgtmp"]
    for k in range(ntiles):
        sap, dap = src_fn(k)
        b.op("vector", lambda h, sap=sap: h.tensor_copy(out=tmp[:, 0:width], in_=sap),
             inc="vv", waits=[("dG", b.val("dG"))])
        b.dma("gpsimd", dap, tmp[:, 0:width], inc="dG", waits=[("vv", b.val("vv"))])


def sincos_col(c, phi, outs, outc, t1, t2):
    b = c.b
    ccols = c.sb["ccols"]
    t2b = c.sb["mskf"][:, 0:1]
    b.vop( lambda h: h.tensor_scalar(
        out=t1, in0=phi, scalar1=ccols[:, C_P2J:C_P2J + 1], scalar2=None,
        op0=ALU.mult), inc="vv")
    b.vop( lambda h: h.tensor_mul(out=t2, in0=t1, in1=t1), inc="vv")

    def horner(out, coeffs):
        b.vop( lambda h, out=out, c0=coeffs[0]: h.tensor_scalar(
            out=out, in0=t2, scalar1=ccols[:, c0:c0 + 1],
            scalar2=ccols[:, coeffs[1]:coeffs[1] + 1], op0=ALU.mult, op1=ALU.add),
            inc="vv")
        for ci in coeffs[2:]:
            b.vop( lambda h, out=out: h.tensor_mul(out=out, in0=out, in1=t2),
                 inc="vv")
            b.vop( lambda h, out=out, ci=ci: h.tensor_scalar(
                out=out, in0=out, scalar1=ccols[:, ci:ci + 1], scalar2=None,
                op0=ALU.add), inc="vv")

    horner(outs, [C_N5040, C_1_120, C_N6TH, C_ONE])
    b.vop( lambda h: h.tensor_mul(out=outs, in0=outs, in1=t1), inc="vv")
    horner(outc, [C_1_40320, C_N720, C_1_24, C_NEGH, C_ONE])
    for _ in range(SINCOS_J):
        b.vop( lambda h: h.tensor_mul(out=t1, in0=outc, in1=outc), inc="vv")
        b.vop( lambda h: h.tensor_mul(out=t2b, in0=outs, in1=outs), inc="vv")
        b.vop( lambda h: h.tensor_mul(out=outs, in0=outs, in1=outc), inc="vv")
        b.vop( lambda h: h.tensor_add(out=outs, in0=outs, in1=outs), inc="vv")
        b.vop( lambda h: h.tensor_sub(out=outc, in0=t1, in1=t2b), inc="vv")


def sweep_proj_stash(c, wsrc, n_out_tiles, n_k, rhs_sb, wf, wb_, evs, dst_st,
                     mul_src=None):
    b, io = c.b, c.io
    ev_cond = [None, None]
    for j in range(n_out_tiles):
        wv = wload(c, wsrc, 0, n_k, j * P, P, wf, wb_)
        for tc_i in range(NTC):
            t0 = tc_i * TC
            dv = None
            if mul_src is not None:
                msrc, mbuf = mul_src
                dv = b.dma("sync", mbuf[:], msrc[j * P:(j + 1) * P, t0:t0 + TC],
                           waits=[("vv", b.val("vv"))])
            i, cond = c.bank()
            pt = c.ps[i]
            for k in range(n_k):
                b.op("tensor", lambda h, k=k, pt=pt, t0=t0, wb_=wb_, rhs_sb=rhs_sb,
                     n_k=n_k: h.matmul(
                         out=pt[:, 0:TC], lhsT=wb_[:, k * P:(k + 1) * P],
                         rhs=rhs_sb[:, k * T + t0:k * T + t0 + TC],
                         start=(k == 0), stop=(k == n_k - 1)),
                     inc="mm" if k == n_k - 1 else None,
                     waits=([("vv", wv)] + c.wc(cond)) if k == 0 else ())
            mmv = b.val("mm")
            e = (j * NTC + tc_i) % 2
            ev = evs[e]
            if mul_src is None:
                vv = b.op("vector", lambda h, pt=pt, ev=ev: h.tensor_copy(
                    out=ev[:], in_=pt[:, 0:TC]), inc="vv",
                    waits=[("mm", mmv)] + c.wc(ev_cond[e]))
            else:
                msrc, mbuf = mul_src
                vv = b.op("vector", lambda h, pt=pt, ev=ev, mbuf=mbuf: h.tensor_mul(
                    out=ev[:], in0=mbuf[:], in1=pt[:, 0:TC]), inc="vv",
                    waits=[("mm", mmv), ("dL", dv)] + c.wc(ev_cond[e]))
            c.bank_release(i, "vv", vv)
            gv = b.dma("gpsimd", dst_st[j * P:(j + 1) * P, t0:t0 + TC],
                       ev[:], inc="dG", waits=[("vv", vv)])
            ev_cond[e] = ("dG", gv)
        b.barrier()



def stage1(c):
    nc, b, io = c.nc, c.b, c.io
    XDW = T + KC

    xd = c.sbuf("xd", [P, 4 * XDW], F32)

    c.push()
    xnsb = c.sbuf("xnsb", [P, 8 * T], BF16)
    xc = c.sbuf("xc", [P, 8 * TC], F32)
    xb = c.sbuf("xb", [P, 8 * TC], BF16)
    xq = c.sbuf("xq", [P, 8 * TC], BF16)
    n1w = c.sbuf("n1w", [P, 8], F32)
    n1b = c.sbuf("n1b", [P, 8], F32)
    for k in range(8):
        b.dma("sync", n1w[:, k:k + 1], io["aa_n1_w"][k * P:(k + 1) * P, 0:1])
        b.dma("sync", n1b[:, k:k + 1], io["aa_n1_b"][k * P:(k + 1) * P, 0:1])
    b.barrier()

    for tc_i in range(NTC):
        t0 = tc_i * TC
        for k in range(8):
            b.dma("sync", xc[:, k * TC:(k + 1) * TC],
                  io["x_in"][k * P:(k + 1) * P, t0:t0 + TC])
        dv = b.val("dL")
        for k in range(8):
            b.op("vector", lambda h, k=k: h.tensor_copy(
                out=xb[:, k * TC:(k + 1) * TC], in_=xc[:, k * TC:(k + 1) * TC]),
                inc="vv", waits=[("dL", dv)] if k == 0 else ())
            b.op("scalar", lambda h, k=k: h.activation(
                out=xq[:, k * TC:(k + 1) * TC], in_=xc[:, k * TC:(k + 1) * TC],
                func=AF.Square), inc="ss", waits=[("dL", dv)] if k == 0 else ())
        vv, sv = b.val("vv"), b.val("ss")
        xb_t = [xb[:, k * TC:(k + 1) * TC] for k in range(8)]
        xq_t = [xq[:, k * TC:(k + 1) * TC] for k in range(8)]
        b.op("tensor", lambda h: None, waits=[("vv", vv), ("ss", sv)])
        ln_stats_from_tiles(c, xb_t, xq_t)
        ln_rows_math(c, C_INVD, fused_n1=True)
        rv = b.val("vv")
        b.dma("gpsimd", io["rows1_st"][0:1, t0:t0 + TC],
              c.sb["row_m"][0:1, 0:TC], inc="dG", waits=[("vv", rv)])
        b.dma("gpsimd", io["rows1_st"][1:2, t0:t0 + TC],
              c.sb["row_s1"][0:1, 0:TC], inc="dG")
        if c.dbg == "rows1":
            for ri, rn in enumerate(("row_s0", "row_s1", "rowr", "rowmr", "row_m")):
                b.dma("gpsimd", io["dbg"][ri:ri + 1, t0:t0 + TC],
                      c.sb[rn][0:1, 0:TC], inc="dG", waits=[("vv", rv)])
        ir, r_mm = bcast_row(c, c.sb["rowr"][0:1, 0:TC], [("vv", rv)])
        im, m_mm = bcast_row(c, c.sb["rowmr"][0:1, 0:TC], [("vv", rv)])
        if c.dbg == "rbc" and tc_i == 0:
            b.op("vector", lambda h, ir=ir: h.tensor_copy(
                out=c.sb["dbgtmp"][:, 0:TC], in_=c.ps[ir][:, 0:TC]),
                inc="vv", waits=[("mm", m_mm)])
            b.dma("gpsimd", io["dbg"][0:P, 0:TC], c.sb["dbgtmp"][:, 0:TC],
                  inc="dG", waits=[("vv", b.val("vv"))])
            b.op("vector", lambda h, im=im: h.tensor_copy(
                out=c.sb["dbgtmp"][:, 0:TC], in_=c.ps[im][:, 0:TC]),
                inc="vv", waits=[("dG", b.val("dG"))])
            b.dma("gpsimd", io["dbg"][0:P, TC:2 * TC], c.sb["dbgtmp"][:, 0:TC],
                  inc="dG", waits=[("vv", b.val("vv"))])
        lastv = None
        for k in range(8):
            xck = xc[:, k * TC:(k + 1) * TC]
            b.op("vector", lambda h, xck=xck, ir=ir: h.tensor_mul(
                out=xck, in0=xck, in1=c.ps[ir][:, 0:TC]),
                inc="vv", waits=[("mm", m_mm)] if k == 0 else ())
            b.op("vector", lambda h, xck=xck, im=im: h.tensor_add(
                out=xck, in0=xck, in1=c.ps[im][:, 0:TC]), inc="vv")
            lastv = b.op("vector", lambda h, k=k, xck=xck, t0=t0: h.tensor_scalar(
                out=xnsb[:, k * T + t0:k * T + t0 + TC], in0=xck,
                scalar1=n1w[:, k:k + 1], scalar2=n1b[:, k:k + 1],
                op0=ALU.mult, op1=ALU.add), inc="vv")
            if c.dbg == "xmid":
                b.dma("gpsimd", io["dbg"][k * P:(k + 1) * P, t0:t0 + TC], xck,
                      inc="dG", waits=[("vv", lastv)])
        c.bank_release(ir, "vv", lastv)
        c.bank_release(im, "vv", lastv)
        b.barrier()

    if c.dbg == "xn1":
        for k in range(8):
            for tc_i in range(NTC):
                t0 = tc_i * TC
                tmp = c.sb["dbgtmp"]
                b.op("vector", lambda h, k=k, t0=t0: h.tensor_copy(
                    out=tmp[:, 0:TC], in_=xnsb[:, k * T + t0:k * T + t0 + TC]),
                    inc="vv", waits=[("dG", b.val("dG"))])
                b.dma("gpsimd", io["dbg"][k * P:(k + 1) * P, t0:t0 + TC],
                      tmp[:, 0:TC], inc="dG", waits=[("vv", b.val("vv"))])
        b.barrier()

    if STOP <= 1:
        c.pop()
        return
    wf = c.sbuf("wf", [P, 8 * 512], F32)
    wb_ = c.sbuf("wb", [P, 8 * 512], BF16)
    ev0 = c.sbuf("ev0", [P, TC], BF16)
    ev1 = c.sbuf("ev1", [P, TC], BF16)
    sweep_proj_stash(c, io["aa_Wxg"], 32, 8, xnsb, wf, wb_, [ev0, ev1],
                     io["xg_st"], mul_src=None)

    if STOP <= 2:
        c.pop()
        return
    n2w = c.sbuf("n2w", [P, 4], F32)
    n2b = c.sbuf("n2b", [P, 4], F32)
    for k in range(4):
        b.dma("sync", n2w[:, k:k + 1], io["aa_n2_w"][k * P:(k + 1) * P, 0:1])
        b.dma("sync", n2b[:, k:k + 1], io["aa_n2_b"][k * P:(k + 1) * P, 0:1])
    wv = wload(c, io["aa_Wdin"], 0, 8, 0, 512, wf, wb_)
    b.barrier()
    for ht in range(4):
        b.op("vector", lambda h, ht=ht: h.memset(xd[:, ht * XDW:ht * XDW + KC], 0.0),
             inc="vv")
    for tc_i in range(NTC):
        t0 = tc_i * TC
        banks = []
        for jt in range(4):
            i, cond = c.bank()
            pt = c.ps[i]
            for k in range(8):
                b.op("tensor", lambda h, k=k, jt=jt, pt=pt, t0=t0: h.matmul(
                    out=pt[:, 0:TC], lhsT=wb_[:, k * 512 + jt * P:k * 512 + (jt + 1) * P],
                    rhs=xnsb[:, k * T + t0:k * T + t0 + TC],
                    start=(k == 0), stop=(k == 7)),
                    inc="mm" if k == 7 else None, waits=c.wc(cond) if k == 0 else ())
            banks.append((jt, i, b.val("mm")))
        for jt, i, mmv in banks:
            dst = xd[:, jt * XDW + KC + t0: jt * XDW + KC + t0 + TC]
            b.op("vector", lambda h, dst=dst, i=i: h.tensor_copy(
                out=dst, in_=c.ps[i][:, 0:TC]), inc="vv", waits=[("mm", mmv)])
            b.op("vector", lambda h, jt=jt, i=i: h.tensor_copy(
                out=xb[:, jt * TC:(jt + 1) * TC], in_=c.ps[i][:, 0:TC]), inc="vv")
            sv = b.op("scalar", lambda h, jt=jt: h.activation(
                out=xq[:, jt * TC:(jt + 1) * TC], in_=xb[:, jt * TC:(jt + 1) * TC],
                func=AF.Square), inc="ss", waits=[("vv", b.val("vv"))])
            c.bank_release(i, "ss", sv)
        vv, sv = b.val("vv"), b.val("ss")
        b.op("tensor", lambda h: None, waits=[("vv", vv), ("ss", sv)])
        xb_t = [xb[:, k * TC:(k + 1) * TC] for k in range(4)]
        xq_t = [xq[:, k * TC:(k + 1) * TC] for k in range(4)]
        ln_stats_from_tiles(c, xb_t, xq_t)
        ln_rows_math(c, C_INV512, fused_n1=False)
        rv = b.val("vv")
        ir, r_mm = bcast_row(c, c.sb["rowr"][0:1, 0:TC], [("vv", rv)])
        im, m_mm = bcast_row(c, c.sb["rowmr"][0:1, 0:TC], [("vv", rv)])
        lastv = None
        for jt in range(4):
            dst = xd[:, jt * XDW + KC + t0: jt * XDW + KC + t0 + TC]
            b.op("vector", lambda h, dst=dst, ir=ir: h.tensor_mul(
                out=dst, in0=dst, in1=c.ps[ir][:, 0:TC]),
                inc="vv", waits=[("mm", m_mm)] if jt == 0 else ())
            b.op("vector", lambda h, dst=dst, im=im: h.tensor_add(
                out=dst, in0=dst, in1=c.ps[im][:, 0:TC]), inc="vv")
            lastv = b.op("vector", lambda h, dst=dst, jt=jt: h.tensor_scalar(
                out=dst, in0=dst, scalar1=n2w[:, jt:jt + 1], scalar2=n2b[:, jt:jt + 1],
                op0=ALU.mult, op1=ALU.add), inc="vv")
        c.bank_release(ir, "vv", lastv)
        c.bank_release(im, "vv", lastv)
        b.barrier()

    if c.dbg == "xnd":
        for jt in range(4):
            for tc_i in range(NTC):
                t0 = tc_i * TC
                src = xd[:, jt * XDW + KC + t0: jt * XDW + KC + t0 + TC]
                b.dma("gpsimd", io["dbg"][jt * P:(jt + 1) * P, t0:t0 + TC], src,
                      inc="dG")
        b.barrier()
    c.pop()
    if STOP <= 3:
        return

    c.push()
    lamc = c.sbuf("lamc", [P, 4 * 8], F32)
    LpR = c.sbuf("LpR", [P, 4 * (KC + 1)], F32)
    LpI = c.sbuf("LpI", [P, 4 * (KC + 1)], F32)
    LpRr = c.sbuf("LpRr", [P, 4 * KC], F32)
    LpIr = c.sbuf("LpIr", [P, 4 * KC], F32)
    MRb = c.sbuf("MRb", [P, 4 * KC], BF16)
    MIb = c.sbuf("MIb", [P, 4 * KC], BF16)
    lamTR = c.sbuf("lamTR", [KC, 4 * P], BF16)
    lamTI = c.sbuf("lamTI", [KC, 4 * P], BF16)
    CcR = c.sbuf("CcR", [P, 4 * 512], F32)
    CcI = c.sbuf("CcI", [P, 4 * 512], F32)
    Kt = c.sbuf("Kt", [P, 4 * KC], F32)
    Dcol = c.sbuf("Dcol", [P, 4], F32)
    GR = c.sbuf("GR", [P, 4 * 512], F32)
    GI = c.sbuf("GI", [P, 4 * 512], F32)
    WRb = c.sbuf("WRb", [P, 4 * 512], BF16)
    WIb = c.sbuf("WIb", [P, 4 * 512], BF16)
    tmp1 = c.sbuf("tmp1", [P, 512], F32)
    tmp2 = c.sbuf("tmp2", [P, 512], F32)
    tmp3 = c.sbuf("tmp3", [P, 512], F32)
    xtm = c.sbuf("xtm", [KC, 512], BF16)
    eye = c.sbuf("eye", [P, P], F32)
    yh = c.sbuf("yh", [P, 4 * (T // 2)], F32)
    ybh = c.sbuf("ybh", [P, 4 * (T // 2)], BF16)

    iota = c.sb["iota"]
    b.dma("sync", eye[:], io["c_eye"][:])
    for nt in range(4):
        b.dma("sync", lamc[:, nt * 8 + 0:nt * 8 + 1], io["aa_LR"][nt * P:(nt + 1) * P, 0:1])
        b.dma("sync", lamc[:, nt * 8 + 1:nt * 8 + 2], io["aa_LI"][nt * P:(nt + 1) * P, 0:1])
        b.dma("sync", Dcol[:, nt:nt + 1], io["aa_D"][nt * P:(nt + 1) * P, 0:1])
    b.barrier()
    for nt in range(4):
        LR_ = lamc[:, nt * 8 + 0:nt * 8 + 1]
        LI_ = lamc[:, nt * 8 + 1:nt * 8 + 2]
        b.sop( lambda h, LR_=LR_: h.activation(out=LR_, in_=LR_, func=AF.Exp), inc="ss")
        b.vop( lambda h, LR_=LR_: h.tensor_single_scalar(
            out=LR_, in_=LR_, scalar=c.sb["ccols"][:, C_NEG1:C_NEG1 + 1], op=ALU.mult), inc="vv",
            waits=[("ss", b.val("ss"))])
        b.sop( lambda h, LI_=LI_: h.activation(out=LI_, in_=LI_, func=AF.Exp), inc="ss")
        b.barrier()
        W1 = KC + 1
        pR = LpR[:, nt * W1:(nt + 1) * W1]
        pI = LpI[:, nt * W1:(nt + 1) * W1]
        rRv = LpRr[:, nt * KC:(nt + 1) * KC]
        rIv = LpIr[:, nt * KC:(nt + 1) * KC]
        rad = tmp1[:, 0:1]
        phi = tmp1[:, 1:2]
        b.sop( lambda h, rad=rad, LR_=LR_: h.activation(
            out=rad, in_=LR_, func=AF.Exp), inc="ss")
        b.vop( lambda h, phi=phi, LI_=LI_: h.tensor_copy(out=phi, in_=LI_),
             inc="vv", waits=[("ss", b.val("ss"))])
        sincos_col(c, phi, tmp2[:, 0:1], tmp3[:, 0:1], tmp2[:, 1:2], tmp3[:, 1:2])
        lR = lamc[:, nt * 8 + 5:nt * 8 + 6]
        lI = lamc[:, nt * 8 + 6:nt * 8 + 7]
        b.vop( lambda h, lR=lR, rad=rad: h.tensor_mul(
            out=lR, in0=rad, in1=tmp3[:, 0:1]), inc="vv")
        b.vop( lambda h, lI=lI, rad=rad: h.tensor_mul(
            out=lI, in0=rad, in1=tmp2[:, 0:1]), inc="vv")
        b.vop( lambda h, pR=pR: h.tensor_copy(
            out=pR[:, 0:1], in_=c.sb["ccols"][:, C_ONE:C_ONE + 1]), inc="vv")
        b.vop( lambda h, pI=pI: h.memset(pI[:, 0:1], 0.0), inc="vv")
        b.vop( lambda h, rRv=rRv: h.tensor_copy(
            out=rRv[:, KC - 1:KC], in_=c.sb["ccols"][:, C_ONE:C_ONE + 1]), inc="vv")
        b.vop( lambda h, rIv=rIv: h.memset(rIv[:, KC - 1:KC], 0.0), inc="vv")
        s1 = tmp1[:, 2:3]
        s2 = tmp1[:, 3:4]
        for k in range(1, KC + 1):
            b.vop( lambda h, pR=pR, lR=lR, k=k, s1=s1: h.tensor_mul(
                out=s1, in0=pR[:, k - 1:k], in1=lR), inc="vv")
            b.vop( lambda h, pI=pI, lI=lI, k=k, s2=s2: h.tensor_mul(
                out=s2, in0=pI[:, k - 1:k], in1=lI), inc="vv")
            b.vop( lambda h, pR=pR, k=k, s1=s1, s2=s2: h.tensor_sub(
                out=pR[:, k:k + 1], in0=s1, in1=s2), inc="vv")
            b.vop( lambda h, pR=pR, lI=lI, k=k, s1=s1: h.tensor_mul(
                out=s1, in0=pR[:, k - 1:k], in1=lI), inc="vv")
            b.vop( lambda h, pI=pI, lR=lR, k=k, s2=s2: h.tensor_mul(
                out=s2, in0=pI[:, k - 1:k], in1=lR), inc="vv")
            b.vop( lambda h, pI=pI, k=k, s1=s1, s2=s2: h.tensor_add(
                out=pI[:, k:k + 1], in0=s1, in1=s2), inc="vv")
            if k <= KC - 1:
                b.vop( lambda h, rRv=rRv, pR=pR, k=k: h.tensor_copy(
                    out=rRv[:, KC - 1 - k:KC - k], in_=pR[:, k:k + 1]), inc="vv")
                b.vop( lambda h, rIv=rIv, pI=pI, k=k: h.tensor_copy(
                    out=rIv[:, KC - 1 - k:KC - k], in_=pI[:, k:k + 1]), inc="vv")
        vv = b.val("vv")
        if c.dbg == "lamchk" and nt == 0:
            dtm = c.sb["dbgtmp"]
            for name, ap, r0 in [("lam56", lamc[:, 0:8], 0),
                                 ("pR", pR, 128), ("pI", pI, 256),
                                 ("t1", tmp1[:, 0:8], 384),
                                 ("t2", tmp2[:, 0:2], 512),
                                 ("t3", tmp3[:, 0:2], 512)]:
                w = ap.shape[-1]
                co = 16 if name == "t3" else 0
                b.vop( lambda h, ap=ap, w=w: h.tensor_copy(
                    out=dtm[:, 0:w], in_=ap), inc="vv", waits=[("dG", b.val("dG"))])
                b.dma("gpsimd", io["dbg"][r0:r0 + P, co:co + w], dtm[:, 0:w],
                      inc="dG", waits=[("vv", b.val("vv"))])
        aR = lamc[:, nt * 8 + 2:nt * 8 + 3]
        aI = lamc[:, nt * 8 + 3:nt * 8 + 4]
        naI = lamc[:, nt * 8 + 4:nt * 8 + 5]
        b.vop( lambda h, aR=aR, pR=pR: h.tensor_copy(out=aR, in_=pR[:, KC:KC + 1]),
             inc="vv", waits=[("vv", vv)])
        b.vop( lambda h, aI=aI, pI=pI: h.tensor_copy(out=aI, in_=pI[:, KC:KC + 1]),
             inc="vv")
        b.vop( lambda h, naI=naI, pI=pI: h.tensor_scalar(
            out=naI, in0=pI[:, KC:KC + 1], scalar1=c.sb["ccols"][:, C_NEG1:C_NEG1 + 1],
            scalar2=None, op0=ALU.mult), inc="vv")
        b.vop( lambda h, nt=nt, pR=pR: h.tensor_copy(
            out=MRb[:, nt * KC:(nt + 1) * KC], in_=pR[:, 1:KC + 1]), inc="vv")
        b.vop( lambda h, nt=nt, pI=pI: h.tensor_scalar(
            out=MIb[:, nt * KC:(nt + 1) * KC], in0=pI[:, 1:KC + 1],
            scalar1=c.sb["ccols"][:, C_NEG1:C_NEG1 + 1],
            scalar2=None, op0=ALU.mult), inc="vv")
        vvx = b.val("vv")
        i0, cd0 = c.bank()
        b.op("tensor", lambda h, i0=i0, rRv=rRv: h.transpose(
            out=c.ps[i0][0:KC, 0:P], in_=rRv, identity=eye[:]),
            inc="mm", waits=[("vv", vvx)] + c.wc(cd0))
        i1, cd1 = c.bank()
        mm1 = b.op("tensor", lambda h, i1=i1, rIv=rIv: h.transpose(
            out=c.ps[i1][0:KC, 0:P], in_=rIv, identity=eye[:]),
            inc="mm", waits=c.wc(cd1))
        b.vop( lambda h, nt=nt, i0=i0: h.tensor_copy(
            out=lamTR[0:KC, nt * P:(nt + 1) * P], in_=c.ps[i0][0:KC, 0:P]),
            inc="vv", waits=[("mm", mm1)])
        vv = b.vop( lambda h, nt=nt, i1=i1: h.tensor_copy(
            out=lamTI[0:KC, nt * P:(nt + 1) * P], in_=c.ps[i1][0:KC, 0:P]), inc="vv")
        c.bank_release(i0, "vv", vv)
        c.bank_release(i1, "vv", vv)
        if DSS_NO_CC:
            b.barrier()
            continue
        l1R = lamc[:, nt * 8 + 5:nt * 8 + 6]
        l1I = lamc[:, nt * 8 + 6:nt * 8 + 7]
        den = lamc[:, nt * 8 + 7:nt * 8 + 8]
        b.vop( lambda h, l1R=l1R, pR=pR: h.tensor_scalar(
            out=l1R, in0=pR[:, 1:2], scalar1=c.sb["ccols"][:, C_NEG1:C_NEG1 + 1], scalar2=None, op0=ALU.add),
            inc="vv")
        b.vop( lambda h, l1I=l1I, pI=pI: h.tensor_copy(out=l1I, in_=pI[:, 1:2]),
             inc="vv")
        b.vop( lambda h, den=den, LR_=LR_: h.tensor_mul(out=den, in0=LR_, in1=LR_),
             inc="vv")
        b.vop( lambda h, LI_=LI_: h.tensor_mul(out=tmp1[:, 0:1], in0=LI_, in1=LI_),
             inc="vv")
        b.vop( lambda h, den=den: h.tensor_add(out=den, in0=den, in1=tmp1[:, 0:1]),
             inc="vv")
        b.vop( lambda h, den=den: h.reciprocal(out=den, in_=den), inc="vv")
        b.vop( lambda h, l1R=l1R, LR_=LR_: h.tensor_mul(
            out=tmp1[:, 0:1], in0=l1R, in1=LR_), inc="vv")
        b.vop( lambda h, l1I=l1I, LI_=LI_: h.tensor_mul(
            out=tmp2[:, 0:1], in0=l1I, in1=LI_), inc="vv")
        b.vop( lambda h: h.tensor_add(out=tmp1[:, 0:1], in0=tmp1[:, 0:1],
                                              in1=tmp2[:, 0:1]), inc="vv")
        b.vop( lambda h, l1I=l1I, LR_=LR_: h.tensor_mul(
            out=tmp2[:, 0:1], in0=l1I, in1=LR_), inc="vv")
        b.vop( lambda h, l1R=l1R, LI_=LI_: h.tensor_mul(
            out=tmp3[:, 0:1], in0=l1R, in1=LI_), inc="vv")
        b.vop( lambda h: h.tensor_sub(out=tmp2[:, 0:1], in0=tmp2[:, 0:1],
                                              in1=tmp3[:, 0:1]), inc="vv")
        fR, fI = l1R, l1I
        b.vop( lambda h, fR=fR, den=den: h.tensor_mul(
            out=fR, in0=tmp1[:, 0:1], in1=den), inc="vv")
        b.vop( lambda h, fI=fI, den=den: h.tensor_mul(
            out=fI, in0=tmp2[:, 0:1], in1=den), inc="vv")
        scr_v = b.val("vv")
        b.dma("sync", tmp1[:, 0:512], io["aa_CRt"][nt * P:(nt + 1) * P, 0:512],
              waits=[("vv", scr_v)])
        b.dma("sync", tmp2[:, 0:512], io["aa_CIt"][nt * P:(nt + 1) * P, 0:512])
        dv = b.val("dL")
        ccR = CcR[:, nt * 512:(nt + 1) * 512]
        ccI = CcI[:, nt * 512:(nt + 1) * 512]
        nfI = den
        b.vop( lambda h, nfI=nfI, fI=fI: h.tensor_scalar(
            out=nfI, in0=fI, scalar1=c.sb["ccols"][:, C_NEG1:C_NEG1 + 1],
            scalar2=None, op0=ALU.mult), inc="vv")
        b.vop( lambda h, ccR=ccR, fR=fR: h.tensor_scalar(
            out=ccR, in0=tmp1[:, 0:512], scalar1=fR, scalar2=None, op0=ALU.mult),
            inc="vv", waits=[("dL", dv), ("vv", b.val("vv"))])
        b.vop( lambda h, ccR=ccR, nfI=nfI: h.scalar_tensor_tensor(
            out=ccR, in0=tmp2[:, 0:512], scalar=nfI, in1=ccR,
            op0=ALU.mult, op1=ALU.add), inc="vv")
        b.vop( lambda h, ccI=ccI, fI=fI: h.tensor_scalar(
            out=ccI, in0=tmp1[:, 0:512], scalar1=fI, scalar2=None, op0=ALU.mult), inc="vv")
        b.vop( lambda h, ccI=ccI, fR=fR: h.scalar_tensor_tensor(
            out=ccI, in0=tmp2[:, 0:512], scalar=fR, in1=ccI, op0=ALU.mult, op1=ALU.add),
            inc="vv")
        b.barrier()
    if DSS_NO_CC or DSS_NO_K:
        c.pop()
        return
    for htile in range(4):
        i0, cd0 = c.bank()
        pt = c.ps[i0]
        nmm = 0
        for nt in range(4):
            b.op("tensor", lambda h, nt=nt, htile=htile, pt=pt: h.matmul(
                out=pt[0:P, 0:KC], lhsT=CcR[:, nt * 512 + htile * P:nt * 512 + (htile + 1) * P],
                rhs=LpR[:, nt * (KC + 1):nt * (KC + 1) + KC],
                start=(nt == 0), stop=False), inc=None,
                waits=c.wc(cd0) if nt == 0 else ())
            nmm += 1
        for nt in range(4):
            b.vop( lambda h, nt=nt: h.tensor_scalar(
                out=tmp1[:, nt * KC:(nt + 1) * KC],
                in0=LpI[:, nt * (KC + 1):nt * (KC + 1) + KC],
                scalar1=c.sb["ccols"][:, C_NEG1:C_NEG1 + 1], scalar2=None,
                op0=ALU.mult), inc="vv")
            vv = b.val("vv")
            b.op("tensor", lambda h, nt=nt, htile=htile, pt=pt: h.matmul(
                out=pt[0:P, 0:KC], lhsT=CcI[:, nt * 512 + htile * P:nt * 512 + (htile + 1) * P],
                rhs=tmp1[:, nt * KC:(nt + 1) * KC], start=False, stop=(nt == 3)),
                inc="mm" if nt == 3 else None, waits=[("vv", vv)])
        mmv = b.val("mm")
        vv = b.vop( lambda h, htile=htile, pt=pt: h.tensor_copy(
            out=Kt[:, htile * KC:(htile + 1) * KC], in_=pt[0:P, 0:KC]), inc="vv",
            waits=[("mm", mmv)])
        c.bank_release(i0, "vv", vv)
    b.op("vector", lambda h: h.memset(GR[:], 0.0), inc="vv")
    b.op("vector", lambda h: h.memset(GI[:], 0.0), inc="vv")
    b.barrier()

    if STOP <= 4:
        c.pop()
        return
    for ch in range(NCH):
        t0 = ch * KC
        half = ch // (NCH // 2)
        hoff = half * (T // 2)
        if ch == 0:
            b.op("vector", lambda h: h.memset(WRb[:], 0.0), inc="vv")
            b.op("vector", lambda h: h.memset(WIb[:], 0.0), inc="vv")
        else:
            for nt in range(4):
                s = slice(nt * 512, (nt + 1) * 512)
                b.op("vector", lambda h, s=s: h.tensor_mul(
                    out=tmp1[:, 0:512], in0=CcR[:, s], in1=GR[:, s]), inc="vv")
                b.op("vector", lambda h, s=s: h.tensor_mul(
                    out=tmp2[:, 0:512], in0=CcI[:, s], in1=GI[:, s]), inc="vv")
                b.op("vector", lambda h, s=s: h.tensor_sub(
                    out=WRb[:, s], in0=tmp1[:, 0:512], in1=tmp2[:, 0:512]), inc="vv")
                b.op("vector", lambda h, s=s: h.tensor_mul(
                    out=tmp1[:, 0:512], in0=CcR[:, s], in1=GI[:, s]), inc="vv")
                b.op("vector", lambda h, s=s: h.tensor_mul(
                    out=tmp2[:, 0:512], in0=CcI[:, s], in1=GR[:, s]), inc="vv")
                b.op("vector", lambda h, s=s: h.tensor_add(
                    out=WIb[:, s], in0=tmp1[:, 0:512], in1=tmp2[:, 0:512]), inc="vv")
        wready = b.val("vv")
        ybanks = []
        for htile in range(4):
            i0, cd0 = c.bank()
            pt = c.ps[i0]
            for nt in range(4):
                hs = slice(nt * 512 + htile * P, nt * 512 + (htile + 1) * P)
                b.op("tensor", lambda h, hs=hs, nt=nt, pt=pt: h.matmul(
                    out=pt[0:P, 0:KC], lhsT=WRb[:, hs], rhs=MRb[:, nt * KC:(nt + 1) * KC],
                    start=(nt == 0), stop=False),
                    waits=([("vv", wready)] + c.wc(cd0)) if nt == 0 else ())
                b.op("tensor", lambda h, hs=hs, nt=nt, pt=pt: h.matmul(
                    out=pt[0:P, 0:KC], lhsT=WIb[:, hs], rhs=MIb[:, nt * KC:(nt + 1) * KC],
                    start=False, stop=(nt == 3)), inc="mm" if nt == 3 else None)
            ybanks.append((htile, i0, b.val("mm")))
        for htile, i0, mmv in ybanks:
            src_ = xd[:, htile * XDW + KC + t0: htile * XDW + KC + t0 + KC]
            dst = yh[:, htile * (T // 2) + (t0 - hoff): htile * (T // 2) + (t0 - hoff) + KC]
            if DSS_TAIL_ONLY:
                vv = b.op("vector", lambda h, dst=dst, i0=i0: h.tensor_copy(
                    out=dst, in_=c.ps[i0][0:P, 0:KC]),
                    inc="vv", waits=[("mm", mmv)])
            else:
                vv = b.op("vector", lambda h, src=src_, dst=dst, htile=htile, i0=i0:
                          h.scalar_tensor_tensor(
                              out=dst, in0=src, scalar=Dcol[:, htile:htile + 1],
                              in1=c.ps[i0][0:P, 0:KC], op0=ALU.mult, op1=ALU.add),
                          inc="vv", waits=[("mm", mmv)])
            c.bank_release(i0, "vv", vv)
        xbanks = []
        for htile in range(4):
            i0, cd0 = c.bank()
            src = xd[:, htile * XDW + KC + t0: htile * XDW + KC + t0 + KC]
            mmv = b.op("tensor", lambda h, i0=i0, src=src: h.transpose(
                out=c.ps[i0][0:KC, 0:P], in_=src, identity=eye[:]),
                inc="mm", waits=c.wc(cd0))
            vv = b.op("vector", lambda h, htile=htile, i0=i0: h.tensor_copy(
                out=xtm[0:KC, htile * P:(htile + 1) * P], in_=c.ps[i0][0:KC, 0:P]),
                inc="vv", waits=[("mm", mmv)])
            c.bank_release(i0, "vv", vv)
        xtv = b.val("vv")
        for nt in range(4):
            iR, cdR = c.bank()
            iI, cdI = c.bank()
            b.op("tensor", lambda h, nt=nt, iR=iR: h.matmul(
                out=c.ps[iR][0:P, 0:512], lhsT=lamTR[:, nt * P:(nt + 1) * P],
                rhs=xtm[0:KC, 0:512], start=True, stop=True), inc="mm",
                waits=[("vv", xtv)] + c.wc(cdR))
            b.op("tensor", lambda h, nt=nt, iI=iI: h.matmul(
                out=c.ps[iI][0:P, 0:512], lhsT=lamTI[:, nt * P:(nt + 1) * P],
                rhs=xtm[0:KC, 0:512], start=True, stop=True), inc="mm",
                waits=c.wc(cdI))
            mmv = b.val("mm")
            s = slice(nt * 512, (nt + 1) * 512)
            aR = lamc[:, nt * 8 + 2:nt * 8 + 3]
            aI = lamc[:, nt * 8 + 3:nt * 8 + 4]
            naI = lamc[:, nt * 8 + 4:nt * 8 + 5]
            b.op("vector", lambda h, s=s, aR=aR, iR=iR: h.scalar_tensor_tensor(
                out=tmp1[:, 0:512], in0=GR[:, s], scalar=aR, in1=c.ps[iR][0:P, 0:512],
                op0=ALU.mult, op1=ALU.add), inc="vv", waits=[("mm", mmv)])
            b.op("vector", lambda h, s=s, aI=aI, iI=iI: h.scalar_tensor_tensor(
                out=tmp2[:, 0:512], in0=GR[:, s], scalar=aI, in1=c.ps[iI][0:P, 0:512],
                op0=ALU.mult, op1=ALU.add), inc="vv")
            b.op("vector", lambda h, s=s, naI=naI: h.scalar_tensor_tensor(
                out=GR[:, s], in0=GI[:, s], scalar=naI, in1=tmp1[:, 0:512],
                op0=ALU.mult, op1=ALU.add), inc="vv")
            vv = b.op("vector", lambda h, s=s, aR=aR: h.scalar_tensor_tensor(
                out=GI[:, s], in0=GI[:, s], scalar=aR, in1=tmp2[:, 0:512],
                op0=ALU.mult, op1=ALU.add), inc="vv")
            c.bank_release(iR, "vv", vv)
            c.bank_release(iI, "vv", vv)
        b.barrier()
        if (ch + 1) % (NCH // 2) == 0:
            H = T // 2
            NCH2 = H // KC
            for htile in range(4):
                if DSS_NO_TAPS:
                    break
                for l in range(KC):
                    xwin = xd[:, htile * XDW + KC + hoff: htile * XDW + KC + hoff + H].rearrange(
                        "p (c t) -> p c t", c=NCH2, t=KC)[:, :, 0:KC - l]
                    dst = yh[:, htile * H:(htile + 1) * H].rearrange(
                        "p (c t) -> p c t", c=NCH2, t=KC)[:, :, l:KC]
                    b.op("vector", lambda h, xwin=xwin, dst=dst, htile=htile, l=l:
                         h.scalar_tensor_tensor(
                             out=dst, in0=xwin, scalar=Kt[:, htile * KC + l:htile * KC + l + 1],
                             in1=dst, op0=ALU.mult, op1=ALU.add), inc="vv")
            vv = b.val("vv")
            for htile in range(4):
                b.op("vector", lambda h, htile=htile: h.tensor_copy(
                    out=ybh[:, htile * H:(htile + 1) * H],
                    in_=yh[:, htile * H:(htile + 1) * H]), inc="vv")
                b.dma("gpsimd", io["ydss_st"][htile * P:(htile + 1) * P, hoff:hoff + H],
                      ybh[:, htile * H:(htile + 1) * H], inc="dG",
                      waits=[("vv", b.val("vv"))])
            b.barrier()

    if c.dbg == "dsschk":
        dt = c.sb["dbgtmp"]
        outs_l = [("Kt", Kt[:, 0:4 * KC], 0), ("LpR", LpR[:, 0:4 * (KC + 1)], 0),
                  ("CcR", CcR[:, 0:512], 0), ("GR", GR[:, 0:512], 128),
                  ("WRb", WRb[:, 0:512], 256), ("MRb", MRb[:, 0:4 * KC], 384),
                  ("lamTRx", lamTR[0:KC, 0:512], 384)]
        col = 0
        rowoff = {"Kt": (0, 0), "LpR": (0, 300), "CcR": (128, 0), "GR": (256, 0),
                  "WRb": (384, 0), "MRb": (0, 600), "lamTRx": (512, 0)}
        for name, ap, _ in outs_l:
            r0, c0 = rowoff[name]
            w = ap.shape[-1]
            np_ = ap.shape[0]
            b.op("vector", lambda h, ap=ap, w=w, np_=np_: h.tensor_copy(
                out=dt[0:np_, 0:w], in_=ap), inc="vv", waits=[("dG", b.val("dG"))])
            b.dma("gpsimd", io["dbg"][r0:r0 + np_, c0:c0 + w], dt[0:np_, 0:w],
                  inc="dG", waits=[("vv", b.val("vv"))])
        b.barrier()
    if c.dbg == "ydss":
        ytb = c.sbuf("ytb", [P, TC], BF16)
        for jt in range(4):
            for tc_i in range(NTC):
                t0 = tc_i * TC
                tmp = c.sb["dbgtmp"]
                dv = b.dma("sync", ytb[:], io["ydss_st"][jt * P:(jt + 1) * P, t0:t0 + TC],
                           waits=[("dG", b.val("dG")), ("vv", b.val("vv"))])
                b.op("vector", lambda h: h.tensor_copy(out=tmp[:, 0:TC], in_=ytb[:]),
                     inc="vv", waits=[("dL", dv)])
                b.dma("gpsimd", io["dbg"][jt * P:(jt + 1) * P, t0:t0 + TC],
                      tmp[:, 0:TC], inc="dG", waits=[("vv", b.val("vv"))])
        b.barrier()
    c.pop()

    if STOP <= 5:
        return
    c.push()
    ydsb = c.sbuf("ydsb", [P, 4 * T], BF16)
    for jt in range(4):
        for tc_i in range(NTC):
            t0 = tc_i * TC
            b.dma("sync", ydsb[:, jt * T + t0: jt * T + t0 + TC],
                  io["ydss_st"][jt * P:(jt + 1) * P, t0:t0 + TC])
    b.barrier()
    wfC = c.sbuf("wf2", [P, 4 * P], F32)
    wbC = c.sbuf("wb2", [P, 4 * P], BF16)
    xgb = c.sbuf("xgb", [P, TC], BF16)
    gev0 = c.sbuf("gev0", [P, TC], BF16)
    gev1 = c.sbuf("gev1", [P, TC], BF16)
    sweep_proj_stash(c, io["aa_Wgate"], 32, 4, ydsb, wfC, wbC, [gev0, gev1],
                     io["gxg_st"], mul_src=(io["xg_st"], xgb))
    c.pop()

    if STOP <= 6:
        return
    c.push()
    wallb = c.sbuf("wallb", [P, 32 * 1024], BF16)
    gx = c.sbuf("gx2", [P, 32 * TC], BF16)
    xc2 = c.sbuf("xc3", [P, 8 * TC], F32)
    wf3 = c.sbuf("wf4", [P, 1024], F32)
    bout = c.sbuf("boutc2", [P, 8], F32)
    for k in range(8):
        b.dma("sync", bout[:, k:k + 1], io["aa_bout"][k * P:(k + 1) * P, 0:1])
    for k in range(32):
        b.dma("sync", wf3[:], io["aa_Wout"][k * P:(k + 1) * P, 0:D])
        dv = b.val("dL")
        b.op("vector", lambda h, k=k: h.tensor_copy(
            out=wallb[:, k * D:(k + 1) * D], in_=wf3[:]), inc="vv",
            waits=[("dL", dv)])
        b.barrier()
    xad = c.sbuf("xad", [P, TC], F32)
    for tc_i in range(NTC):
        t0 = tc_i * TC
        for k in range(32):
            b.dma("sync", gx[:, k * TC:(k + 1) * TC],
                  io["gxg_st"][k * P:(k + 1) * P, t0:t0 + TC])
        for k in range(8):
            b.dma("sync", xc2[:, k * TC:(k + 1) * TC],
                  io["x_in"][k * P:(k + 1) * P, t0:t0 + TC])
        dv = b.val("dL")
        b.op("tensor", lambda h: None, waits=[("dL", dv)])
        rv = rows_from_mv(c, io["rows1_st"], t0)
        ir, r_mm = bcast_row(c, c.sb["rowr"][0:1, 0:TC], [("vv", rv)])
        im, m_mm = bcast_row(c, c.sb["rowmr"][0:1, 0:TC], [("vv", rv)])
        lastv = None
        for ot in range(8):
            i, cond = c.bank()
            pt = c.ps[i]
            for k in range(32):
                b.op("tensor", lambda h, k=k, ot=ot, pt=pt: h.matmul(
                    out=pt[:, 0:TC], lhsT=wallb[:, k * D + ot * P: k * D + (ot + 1) * P],
                    rhs=gx[:, k * TC:(k + 1) * TC], start=(k == 0), stop=(k == 31)),
                    inc="mm" if k == 31 else None,
                    waits=([("mm", m_mm)] + c.wc(cond)) if k == 0 else ())
            mmv = b.val("mm")
            xck = xc2[:, ot * TC:(ot + 1) * TC]
            b.op("vector", lambda h, xck=xck, ir=ir: h.tensor_mul(
                out=xad[:], in0=xck, in1=c.ps[ir][:, 0:TC]), inc="vv",
                waits=[("mm", mmv)])
            b.op("vector", lambda h, im=im: h.tensor_add(
                out=xad[:], in0=xad[:], in1=c.ps[im][:, 0:TC]), inc="vv")
            b.op("vector", lambda h, xck=xck, pt=pt: h.tensor_add(
                out=xck, in0=xck, in1=pt[:, 0:TC]), inc="vv")
            b.op("vector", lambda h, xck=xck, ot=ot: h.tensor_scalar(
                out=xck, in0=xck, scalar1=bout[:, ot:ot + 1], scalar2=None,
                op0=ALU.add), inc="vv")
            vv = b.op("vector", lambda h, xck=xck: h.tensor_add(
                out=xck, in0=xck, in1=xad[:]), inc="vv")
            lastv = vv
            c.bank_release(i, "vv", vv)
            b.dma("gpsimd", io["xres"][ot * P:(ot + 1) * P, t0:t0 + TC], xck,
                  inc="dG", waits=[("vv", vv)])
        c.bank_release(ir, "vv", lastv)
        c.bank_release(im, "vv", lastv)
        b.barrier()
    c.pop()





def ln_driver_sweep(c, src_hbm, dst_hbm):
    b, io = c.b, c.io
    c.push()
    xc = c.sbuf("lnxc", [P, 8 * TC], F32)
    xb = c.sbuf("lnxb", [P, 8 * TC], BF16)
    xq = c.sbuf("lnxq", [P, 8 * TC], BF16)
    for tc_i in range(NTC):
        t0 = tc_i * TC
        for k in range(8):
            b.dma("sync", xc[:, k * TC:(k + 1) * TC],
                  src_hbm[k * P:(k + 1) * P, t0:t0 + TC])
        dv = b.val("dL")
        for k in range(8):
            b.op("vector", lambda h, k=k: h.tensor_copy(
                out=xb[:, k * TC:(k + 1) * TC], in_=xc[:, k * TC:(k + 1) * TC]),
                inc="vv", waits=[("dL", dv)] if k == 0 else ())
            b.op("scalar", lambda h, k=k: h.activation(
                out=xq[:, k * TC:(k + 1) * TC], in_=xc[:, k * TC:(k + 1) * TC],
                func=AF.Square), inc="ss", waits=[("dL", dv)] if k == 0 else ())
        vv, sv = b.val("vv"), b.val("ss")
        b.op("tensor", lambda h: None, waits=[("vv", vv), ("ss", sv)])
        ln_stats_from_tiles(c, [xb[:, k * TC:(k + 1) * TC] for k in range(8)],
                            [xq[:, k * TC:(k + 1) * TC] for k in range(8)])
        ln_rows_math(c, C_INVD, fused_n1=False)
        rv = b.val("vv")
        ir, r_mm = bcast_row(c, c.sb["rowr"][0:1, 0:TC], [("vv", rv)])
        im, m_mm = bcast_row(c, c.sb["rowmr"][0:1, 0:TC], [("vv", rv)])
        lastv = None
        for k in range(8):
            xck = xc[:, k * TC:(k + 1) * TC]
            b.op("vector", lambda h, xck=xck, ir=ir: h.tensor_mul(
                out=xck, in0=xck, in1=c.ps[ir][:, 0:TC]),
                inc="vv", waits=[("mm", m_mm)] if k == 0 else ())
            lastv = b.op("vector", lambda h, xck=xck, im=im: h.tensor_add(
                out=xck, in0=xck, in1=c.ps[im][:, 0:TC]), inc="vv")
            b.dma("gpsimd", dst_hbm[k * P:(k + 1) * P, t0:t0 + TC], xck,
                  inc="dG", waits=[("vv", lastv)])
        c.bank_release(ir, "vv", lastv)
        c.bank_release(im, "vv", lastv)
        b.barrier()
    c.pop()


def add_sweep(c, dst_hbm, add_hbm):
    b = c.b
    c.push()
    xa = c.sbuf("adda", [P, 8 * TC], F32)
    xb_ = c.sbuf("addb", [P, 8 * TC], F32)
    for tc_i in range(NTC):
        t0 = tc_i * TC
        for k in range(8):
            b.dma("sync", xa[:, k * TC:(k + 1) * TC],
                  dst_hbm[k * P:(k + 1) * P, t0:t0 + TC])
            b.dma("sync", xb_[:, k * TC:(k + 1) * TC],
                  add_hbm[k * P:(k + 1) * P, t0:t0 + TC])
        dv = b.val("dL")
        for k in range(8):
            vv = b.op("vector", lambda h, k=k: h.tensor_add(
                out=xa[:, k * TC:(k + 1) * TC], in0=xa[:, k * TC:(k + 1) * TC],
                in1=xb_[:, k * TC:(k + 1) * TC]),
                inc="vv", waits=[("dL", dv)] if k == 0 else ())
            b.dma("gpsimd", dst_hbm[k * P:(k + 1) * P, t0:t0 + TC],
                  xa[:, k * TC:(k + 1) * TC], inc="dG", waits=[("vv", vv)])
        b.barrier()
    c.pop()


def s2_attn(c, l):
    b, io = c.b, c.io
    c.push()
    wqb = c.sbuf("wqb", [P, 8 * 1536], BF16)
    wob = c.sbuf("wob", [P, 4 * D], BF16)
    wfs = c.sbuf("wfs", [P, 1536], F32)
    lamask = c.sbuf("lamask", [P, 8 * 512], F32)
    ctab = c.sbuf("ctab", [P, 4 * TC], F32)
    xc = c.sbuf("axc", [P, 8 * (TC + 1)], F32)
    hb = c.sbuf("ahb", [P, 8 * TC], BF16)
    qb = c.sbuf("aqb", [P, 4 * TC], BF16)
    kring = c.sbuf("akring", [P, 4 * 2 * TC], BF16)
    vring = c.sbuf("avring", [P, 8 * TC], BF16)
    qf = c.sbuf("aqf", [P, TC], F32)
    sqb = c.sbuf("asqb", [P, TC], BF16)
    r2 = c.sbuf("ar2", [2, TC], F32)
    nmr = c.sbuf("anmr", [1, TC], F32)
    nms = c.sbuf("anms", [P, TC], F32)
    eb = c.sbuf("aeb", [P, 8 * TC], BF16)
    etmp = c.sbuf("aetmp", [P, TC], F32)
    of32 = c.sbuf("aof32", [64, TC], F32)
    obuf = c.sbuf("aob", [P, 4 * TC], BF16)
    rwrow = c.sbuf("arw", [1, TC], F32)
    qs_c = c.sbuf("aqs", [P, 2], F32)
    blkb = c.sbuf("ablkb", [P, 2], BF16)
    blkf = c.sbuf("ablkf", [P, 2], F32)
    bkT = c.sbuf("abkT", [2, P], F32)
    swp = c.sbuf("aswp", [P, P], F32)

    b.dma("sync", qs_c[:, 0:1], io[f"ff_qs{l}"][:, 0:1])
    b.dma("sync", qs_c[:, 1:2], io[f"ff_ks{l}"][:, 0:1])
    b.barrier()
    b.vop(lambda h: h.tensor_single_scalar(
        out=qs_c[:, 0:1], in_=qs_c[:, 0:1],
        scalar=c.sb["ccols"][:, C_EIGHT:C_EIGHT + 1], op=ALU.mult))
    b.dma("sync", blkf[:], io["c_blk64"][:])
    b.dma("sync", bkT[:], io["c_blk64T"][:])
    b.dma("sync", swp[:], io["c_swap"][:])
    for kc in range(8):
        b.dma("sync", lamask[:, kc * 512:(kc + 1) * 512],
              io["c_lamask"][kc * P:(kc + 1) * P, 0:512])
    for dk in range(8):
        b.dma("sync", wfs[:], io[f"ff_wqkv{l}"][dk * P:(dk + 1) * P, 0:1536])
        dv = b.val("dL")
        b.op("vector", lambda h, dk=dk: h.tensor_copy(
            out=wqb[:, dk * 1536:(dk + 1) * 1536], in_=wfs[:]), inc="vv",
            waits=[("dL", dv)])
        b.barrier()
    for kt in range(4):
        b.dma("sync", wfs[:, 0:D], io[f"ff_wo{l}"][kt * P:(kt + 1) * P, 0:D])
        dv = b.val("dL")
        b.op("vector", lambda h, kt=kt: h.tensor_copy(
            out=wob[:, kt * D:(kt + 1) * D], in_=wfs[:, 0:D]), inc="vv",
            waits=[("dL", dv)])
        b.barrier()
    b.op("vector", lambda h: h.tensor_copy(out=blkb[:], in_=blkf[:]), inc="vv")
    b.barrier()

    TCP = TC + 1
    for tc_i in range(NTC):
        t0 = tc_i * TC
        for k in range(8):
            if tc_i == 0:
                b.dma("sync", xc[:, k * TCP + 1:k * TCP + TCP],
                      io["xi_st"][k * P:(k + 1) * P, 0:TC])
            else:
                b.dma("sync", xc[:, k * TCP:k * TCP + TCP],
                      io["xi_st"][k * P:(k + 1) * P, t0 - 1:t0 + TC])
        for ti, tn in enumerate(("c_rotcq", "c_rotsq", "c_rotck", "c_rotsk")):
            b.dma("sync", ctab[:, ti * TC:(ti + 1) * TC], io[tn][:, t0:t0 + TC])
        dv = b.val("dL")
        if tc_i == 0:
            for k in range(8):
                b.op("vector", lambda h, k=k: h.memset(
                    xc[:, k * TCP:k * TCP + 1], 0.0), inc="vv",
                    waits=[("dL", dv)] if k == 0 else ())
        for k in range(8):
            off = 1 if k < 4 else 0
            b.op("vector", lambda h, k=k, off=off: h.tensor_copy(
                out=hb[:, k * TC:(k + 1) * TC],
                in_=xc[:, k * TCP + off:k * TCP + off + TC]),
                inc="vv", waits=[("dL", dv)] if (k == 0 and tc_i > 0) else ())
        hv = b.val("vv")
        b.op("tensor", lambda h: None, waits=[("vv", hv)])

        for ot in range(8):
            i, cond = c.bank()
            pt = c.ps[i]
            for dk in range(8):
                b.op("tensor", lambda h, dk=dk, ot=ot, pt=pt: h.matmul(
                    out=pt[:, 0:TC], lhsT=wqb[:, dk * 1536 + ot * P: dk * 1536 + (ot + 1) * P],
                    rhs=hb[:, dk * TC:(dk + 1) * TC], start=(dk == 0), stop=(dk == 7)),
                    inc="mm" if dk == 7 else None, waits=c.wc(cond) if dk == 0 else ())
            mmv = b.val("mm")
            is_q = ot < 4
            b.op("vector", lambda h, pt=pt: h.tensor_copy(out=qf[:], in_=pt[:, 0:TC]),
                 inc="vv", waits=[("mm", mmv)])
            vv = b.val("vv")
            c.bank_release(i, "vv", vv)
            b.op("scalar", lambda h: h.activation(out=sqb[:], in_=qf[:], func=AF.Square),
                 inc="ss", waits=[("vv", vv)])
            sv = b.val("ss")
            i2, cond2 = c.bank()
            p2 = c.ps[i2]
            mmv = b.op("tensor", lambda h, p2=p2: h.matmul(
                out=p2[0:2, 0:TC], lhsT=blkb[:], rhs=sqb[:], start=True, stop=True),
                inc="mm", waits=[("ss", sv)] + c.wc(cond2))
            b.vop(lambda h, p2=p2: h.tensor_copy(out=r2[:], in_=p2[0:2, 0:TC]),
                  waits=[("mm", mmv)])
            vv = b.val("vv")
            c.bank_release(i2, "vv", vv)
            b.sop(lambda h: h.activation(out=r2[:], in_=r2[:], func=AF.Ln),
                  waits=[("vv", vv)])
            b.sop(lambda h: h.activation(out=r2[:], in_=r2[:], func=AF.Exp,
                                         scale=c.sb["ccols"][0:2, C_NEGH:C_NEGH + 1]))
            b.vop(lambda h: h.tensor_single_scalar(
                out=r2[:], in_=r2[:], scalar=c.sb["ccols"][0:2, C_E12:C_E12 + 1],
                op=ALU.min), waits=[("ss", b.val("ss"))])
            rv = b.val("vv")
            i3, cond3 = c.bank()
            p3 = c.ps[i3]
            mm3 = b.op("tensor", lambda h, p3=p3: h.matmul(
                out=p3[:, 0:TC], lhsT=bkT[:], rhs=r2[:], start=True, stop=True),
                inc="mm", waits=[("vv", rv)] + c.wc(cond3))
            sc = qs_c[:, 0:1] if is_q else qs_c[:, 1:2]
            b.op("vector", lambda h, p3=p3, sc=sc: h.scalar_tensor_tensor(
                out=qf[:], in0=qf[:], scalar=sc, in1=p3[:, 0:TC],
                op0=ALU.mult, op1=ALU.mult), inc="vv", waits=[("mm", mm3)])
            vv = b.val("vv")
            c.bank_release(i3, "vv", vv)
            cslice = ctab[:, 0:TC] if is_q else ctab[:, 2 * TC:3 * TC]
            sslice = ctab[:, TC:2 * TC] if is_q else ctab[:, 3 * TC:4 * TC]
            i4, cond4 = c.bank()
            p4 = c.ps[i4]
            mm4 = b.op("tensor", lambda h, p4=p4: h.matmul(
                out=p4[:, 0:TC], lhsT=swp[:], rhs=qf[:], start=True, stop=True),
                inc="mm", waits=[("vv", vv)] + c.wc(cond4))
            b.op("vector", lambda h, cslice=cslice: h.tensor_mul(
                out=etmp[:], in0=qf[:], in1=cslice), inc="vv")
            b.op("vector", lambda h, p4=p4, sslice=sslice: h.tensor_mul(
                out=qf[:], in0=p4[:, 0:TC], in1=sslice), inc="vv",
                waits=[("mm", mm4)])
            if is_q:
                dst = qb[:, ot * TC:(ot + 1) * TC]
            else:
                slot = tc_i % 2
                dst = kring[:, (ot - 4) * 2 * TC + slot * TC:
                            (ot - 4) * 2 * TC + slot * TC + TC]
            vv = b.op("vector", lambda h, dst=dst: h.tensor_add(
                out=dst, in0=etmp[:], in1=qf[:]), inc="vv")
            c.bank_release(i4, "vv", vv)
        for sub in range(4):
            i, cond = c.bank()
            pt = c.ps[i]
            for dk in range(8):
                b.op("tensor", lambda h, dk=dk, sub=sub, pt=pt: h.matmul(
                    out=pt[:, 0:512],
                    lhsT=hb[:, dk * TC + sub * P: dk * TC + (sub + 1) * P],
                    rhs=wqb[:, dk * 1536 + 1024: dk * 1536 + 1536],
                    start=(dk == 0), stop=(dk == 7)),
                    inc="mm" if dk == 7 else None, waits=c.wc(cond) if dk == 0 else ())
            mmv = b.val("mm")
            slot = (tc_i * 4 + sub) % 8
            vv = b.op("vector", lambda h, pt=pt, slot=slot: h.tensor_copy(
                out=vring[:, slot * TC:slot * TC + 512], in_=pt[:, 0:512]),
                inc="vv", waits=[("mm", mmv)])
            c.bank_release(i, "vv", vv)
        b.barrier()

        kcs = list(range(8)) if tc_i > 0 else list(range(4, 8))
        for hh in range(8):
            qt, hp = hh // 2, 64 * (hh % 2)
            qhead = qb[hp:hp + 64, qt * TC:(qt + 1) * TC]
            i5, cond5 = c.bank()
            p5 = c.ps[i5]
            first = True
            for half in ((0, 1) if tc_i > 0 else (1,)):
                slot = (tc_i - 1 + half) % 2
                for sub in range(4):
                    i6, cond6 = c.bank()
                    p6 = c.ps[i6]
                    mm6 = b.op("tensor", lambda h, p6=p6, qt=qt, hp=hp, sub=sub,
                               slot=slot: h.matmul(
                        out=p6[:, 0:TC],
                        lhsT=qb[hp:hp + 64, qt * TC + sub * P: qt * TC + (sub + 1) * P],
                        rhs=kring[hp:hp + 64, qt * 2 * TC + slot * TC:
                                  qt * 2 * TC + slot * TC + TC],
                        start=True, stop=True), inc="mm",
                        waits=c.wc(cond6))
                    b.op("vector", lambda h, p6=p6, sub=sub: h.reduce_max(
                        out=etmp[sub * 32:sub * 32 + 1, 0:1] if False else
                        etmp[0:P, sub:sub + 1],
                        in_=p6[:, 0:TC], axis=mybir.AxisListType.X), inc="vv",
                        waits=[("mm", mm6)])
                    vv = b.val("vv")
                    c.bank_release(i6, "vv", vv)
                    if first:
                        b.vop(lambda h, sub=sub: h.tensor_copy(
                            out=etmp[0:P, 4 + sub:5 + sub], in_=etmp[0:P, sub:sub + 1]))
                    else:
                        b.vop(lambda h, sub=sub: h.tensor_max(
                            out=etmp[0:P, 4 + sub:5 + sub],
                            in0=etmp[0:P, 4 + sub:5 + sub],
                            in1=etmp[0:P, sub:sub + 1]))
                first = False
            vvx = b.val("vv")
            for sub in range(4):
                b.op("tensor", lambda h, p5=p5, sub=sub: h.transpose(
                    out=p5[0:1, sub * P:(sub + 1) * P],
                    in_=etmp[0:P, 4 + sub:5 + sub], identity=c.sb["eye2"][:]),
                    inc="mm", waits=([("vv", vvx)] + c.wc(cond5)) if sub == 0 else ())
            mm5 = b.val("mm")
            b.vop(lambda h, p5=p5: h.tensor_copy(out=nmr[:], in_=p5[0:1, 0:TC]),
                  waits=[("mm", mm5)])
            b.vop(lambda h: h.tensor_scalar(
                out=nmr[:], in0=nmr[:], scalar1=c.sb["ccols"][0:1, C_NEG1:C_NEG1 + 1],
                scalar2=None, op0=ALU.mult))
            rv = b.val("vv")
            c.bank_release(i5, "vv", rv)
            i7, cond7 = c.bank()
            p7 = c.ps[i7]
            mm7 = b.op("tensor", lambda h, p7=p7: h.matmul(
                out=p7[:, 0:TC], lhsT=c.sb["ones_row"][0:1, 0:P], rhs=nmr[:],
                start=True, stop=True), inc="mm", waits=[("vv", rv)] + c.wc(cond7))
            vv = b.op("vector", lambda h, p7=p7: h.tensor_copy(
                out=nms[:], in_=p7[:, 0:TC]), inc="vv", waits=[("mm", mm7)])
            c.bank_release(i7, "vv", vv)
            nmsv = vv

            kinfo = []
            for ki, kc in enumerate(kcs):
                if kc < 4:
                    slot = (tc_i - 1) % 2
                    kcol = qt * 2 * TC + slot * TC + (kc % 4) * P
                    vslot = ((tc_i - 1) * 4 + kc) % 8
                else:
                    slot = tc_i % 2
                    kcol = qt * 2 * TC + slot * TC + (kc - 4) * P
                    vslot = (tc_i * 4 + (kc - 4)) % 8
                kinfo.append((kc, vslot))
                i10, cond10 = c.bank()
                p10 = c.ps[i10]
                mm10 = b.op("tensor", lambda h, p10=p10, hp=hp, kcol=kcol, qhead=qhead:
                            h.matmul(out=p10[:, 0:TC],
                                     lhsT=kring[hp:hp + 64, kcol:kcol + P],
                                     rhs=qhead, start=True, stop=True),
                            inc="mm", waits=c.wc(cond10))
                b.op("vector", lambda h, p10=p10: h.tensor_add(
                    out=etmp[:, 0:TC], in0=nms[:], in1=p10[:, 0:TC]), inc="vv",
                    waits=[("mm", mm10), ("vv", nmsv)])
                vv = b.val("vv")
                c.bank_release(i10, "vv", vv)
                b.op("scalar", lambda h: h.activation(
                    out=etmp[:, 0:TC], in_=etmp[:, 0:TC], func=AF.Exp), inc="ss",
                    waits=[("vv", vv)])
                sv = b.val("ss")
                ebk = eb[:, kc * TC:(kc + 1) * TC]
                b.op("vector", lambda h, ebk=ebk, kc=kc: h.tensor_mul(
                    out=ebk, in0=etmp[:, 0:TC],
                    in1=lamask[:, kc * 512:(kc + 1) * 512]), inc="vv",
                    waits=[("ss", sv)])
            ev = b.val("vv")
            i8, cond8 = c.bank()
            p8 = c.ps[i8]
            nk = len(kinfo)
            for ki, (kc, vslot) in enumerate(kinfo):
                ebk = eb[:, kc * TC:(kc + 1) * TC]
                b.op("tensor", lambda h, p8=p8, ebk=ebk, ki=ki, nk=nk: h.matmul(
                    out=p8[0:1, 0:TC], lhsT=c.sb["ones_bf"][:], rhs=ebk,
                    start=(ki == 0), stop=(ki == nk - 1)),
                    inc="mm" if ki == nk - 1 else None,
                    waits=([("vv", ev)] + c.wc(cond8)) if ki == 0 else ())
            i9, cond9 = c.bank()
            p9 = c.ps[i9]
            for ki, (kc, vslot) in enumerate(kinfo):
                ebk = eb[:, kc * TC:(kc + 1) * TC]
                b.op("tensor", lambda h, p9=p9, vslot=vslot, ebk=ebk, ki=ki, hh=hh,
                     nk=nk: h.matmul(
                    out=p9[0:64, 0:TC],
                    lhsT=vring[:, vslot * TC + hh * 64: vslot * TC + hh * 64 + 64],
                    rhs=ebk, start=(ki == 0), stop=(ki == nk - 1)),
                    inc="mm" if ki == nk - 1 else None,
                    waits=c.wc(cond9) if ki == 0 else ())
            mmv = b.val("mm")
            b.vop(lambda h, p8=p8: h.tensor_copy(out=rwrow[:], in_=p8[0:1, 0:TC]),
                  waits=[("mm", mmv)])
            b.vop(lambda h: h.reciprocal(out=rwrow[:], in_=rwrow[:]))
            rv = b.val("vv")
            c.bank_release(i8, "vv", rv)
            i11, cond11 = c.bank()
            p11 = c.ps[i11]
            mm11 = b.op("tensor", lambda h, p11=p11: h.matmul(
                out=p11[0:64, 0:TC], lhsT=c.sb["ones_row"][0:1, 0:64], rhs=rwrow[:],
                start=True, stop=True), inc="mm", waits=[("vv", rv)] + c.wc(cond11))
            b.op("vector", lambda h, p9=p9: h.tensor_copy(
                out=of32[:], in_=p9[0:64, 0:TC]), inc="vv", waits=[("mm", mmv)])
            vv = b.val("vv")
            c.bank_release(i9, "vv", vv)
            dsto = obuf[hp:hp + 64, qt * TC:(qt + 1) * TC]
            vv = b.op("vector", lambda h, p11=p11, dsto=dsto: h.tensor_mul(
                out=dsto, in0=of32[:], in1=p11[0:64, 0:TC]), inc="vv",
                waits=[("mm", mm11)])
            c.bank_release(i11, "vv", vv)
        b.barrier()
        for ot in range(8):
            i, cond = c.bank()
            pt = c.ps[i]
            for kt in range(4):
                b.op("tensor", lambda h, kt=kt, ot=ot, pt=pt: h.matmul(
                    out=pt[:, 0:TC], lhsT=wob[:, kt * D + ot * P: kt * D + (ot + 1) * P],
                    rhs=obuf[:, kt * TC:(kt + 1) * TC], start=(kt == 0), stop=(kt == 3)),
                    inc="mm" if kt == 3 else None, waits=c.wc(cond) if kt == 0 else ())
            mmv = b.val("mm")
            vv = b.op("vector", lambda h, pt=pt, ot=ot: h.tensor_add(
                out=xc[:, ot * TCP + 1: ot * TCP + TCP],
                in0=xc[:, ot * TCP + 1: ot * TCP + TCP], in1=pt[:, 0:TC]),
                inc="vv", waits=[("mm", mmv)])
            c.bank_release(i, "vv", vv)
            b.dma("gpsimd", io["xi_st"][ot * P:(ot + 1) * P, t0:t0 + TC],
                  xc[:, ot * TCP + 1: ot * TCP + TCP], inc="dG",
                  waits=[("vv", vv)])
        b.barrier()
    c.pop()


def s2_ffn(c, l):
    b, io = c.b, c.io
    HIDT = 22
    c.push()
    hn = c.sbuf("fhn", [P, 8 * T], BF16)
    fwk = c.sbuf("fwk", [P, TC], F32)
    fwfa = c.sbuf("fwfa", [P, 16 * P], F32)
    fwba = c.sbuf("fwba", [P, 16 * P], BF16)
    fb1c = c.sbuf("fb1c", [P, 2], F32)
    fgaf = c.sbuf("fgaf", [P, 2 * TC], F32)
    fhp = c.sbuf("fhp", [P, TC], BF16)
    fxc = c.sbuf("fxc", [P, 8 * (TC + 1)], F32)
    fhb = c.sbuf("fhb", [P, 8 * TC], BF16)
    fhq = c.sbuf("fhq", [P, 8 * TC], BF16)
    lngc = c.sbuf("flng", [P, 8], F32)
    lnbc = c.sbuf("flnb", [P, 8], F32)
    for k in range(8):
        b.dma("sync", lngc[:, k:k + 1], io[f"ff_lng{l}"][k * P:(k + 1) * P, 0:1])
        b.dma("sync", lnbc[:, k:k + 1], io[f"ff_lnb{l}"][k * P:(k + 1) * P, 0:1])
    b.barrier()
    TCP = TC + 1
    for tc_i in range(NTC):
        t0 = tc_i * TC
        for k in range(8):
            if tc_i == 0:
                b.dma("sync", fxc[:, k * TCP + 1:k * TCP + TCP],
                      io["xi_st"][k * P:(k + 1) * P, 0:TC])
            else:
                b.dma("sync", fxc[:, k * TCP:k * TCP + TCP],
                      io["xi_st"][k * P:(k + 1) * P, t0 - 1:t0 + TC])
        dv = b.val("dL")
        if tc_i == 0:
            for k in range(8):
                b.op("vector", lambda h, k=k: h.memset(
                    fxc[:, k * TCP:k * TCP + 1], 0.0), inc="vv",
                    waits=[("dL", dv)] if k == 0 else ())
        for k in range(8):
            off = 1 if k < 4 else 0
            b.op("vector", lambda h, k=k, off=off: h.tensor_copy(
                out=fhb[:, k * TC:(k + 1) * TC],
                in_=fxc[:, k * TCP + off:k * TCP + off + TC]),
                inc="vv", waits=[("dL", dv)] if (k == 0 and tc_i > 0) else ())
            b.op("scalar", lambda h, k=k, off=off: h.activation(
                out=fhq[:, k * TC:(k + 1) * TC],
                in_=fxc[:, k * TCP + off:k * TCP + off + TC],
                func=AF.Square), inc="ss",
                waits=[("vv", b.val("vv"))] if k == 0 else ())
        vv, sv = b.val("vv"), b.val("ss")
        b.op("tensor", lambda h: None, waits=[("vv", vv), ("ss", sv)])
        ln_stats_from_tiles(c, [fhb[:, k * TC:(k + 1) * TC] for k in range(8)],
                            [fhq[:, k * TC:(k + 1) * TC] for k in range(8)])
        ln_rows_math(c, C_INVD, fused_n1=False)
        rv = b.val("vv")
        ir, r_mm = bcast_row(c, c.sb["rowr"][0:1, 0:TC], [("vv", rv)])
        im, m_mm = bcast_row(c, c.sb["rowmr"][0:1, 0:TC], [("vv", rv)])
        lastv = None
        for k in range(8):
            hbk = fhb[:, k * TC:(k + 1) * TC]
            b.op("vector", lambda h, hbk=hbk, ir=ir: h.tensor_mul(
                out=fwk[:, 0:TC], in0=hbk, in1=c.ps[ir][:, 0:TC]),
                inc="vv", waits=[("mm", m_mm)] if k == 0 else ())
            b.op("vector", lambda h, im=im: h.tensor_add(
                out=fwk[:, 0:TC], in0=fwk[:, 0:TC],
                in1=c.ps[im][:, 0:TC]), inc="vv")
            lastv = b.op("vector", lambda h, k=k, t0=t0: h.tensor_scalar(
                out=hn[:, k * T + t0:k * T + t0 + TC], in0=fwk[:, 0:TC],
                scalar1=lngc[:, k:k + 1], scalar2=lnbc[:, k:k + 1],
                op0=ALU.mult, op1=ALU.add), inc="vv")
        c.bank_release(ir, "vv", lastv)
        c.bank_release(im, "vv", lastv)
        b.barrier()

    for otp in range(HIDT):
        for dk in range(8):
            b.dma("sync", fwfa[:, dk * P:(dk + 1) * P],
                  io[f"ff_w1a{l}"][dk * P:(dk + 1) * P, otp * P:(otp + 1) * P])
            b.dma("sync", fwfa[:, (8 + dk) * P:(9 + dk) * P],
                  io[f"ff_w1g{l}"][dk * P:(dk + 1) * P, otp * P:(otp + 1) * P])
        b.dma("sync", fb1c[:, 0:1], io[f"ff_b1a{l}"][otp * P:(otp + 1) * P, 0:1])
        b.dma("sync", fb1c[:, 1:2], io[f"ff_b1g{l}"][otp * P:(otp + 1) * P, 0:1])
        dv = b.val("dL")
        wv = None
        for dk in range(16):
            wv = b.op("vector", lambda h, dk=dk: h.tensor_copy(
                out=fwba[:, dk * P:(dk + 1) * P], in_=fwfa[:, dk * P:(dk + 1) * P]),
                inc="vv", waits=[("dL", dv)] if dk == 0 else ())
        for tc_i in range(NTC):
            t0 = tc_i * TC
            ia, conda = c.bank()
            pa = c.ps[ia]
            ig, condg = c.bank()
            pg = c.ps[ig]
            for dk in range(8):
                b.op("tensor", lambda h, dk=dk, pa=pa, t0=t0: h.matmul(
                    out=pa[:, 0:TC], lhsT=fwba[:, dk * P:(dk + 1) * P],
                    rhs=hn[:, dk * T + t0: dk * T + t0 + TC],
                    start=(dk == 0), stop=(dk == 7)),
                    inc="mm" if dk == 7 else None,
                    waits=([("vv", wv)] + c.wc(conda)) if dk == 0 else ())
            for dk in range(8):
                b.op("tensor", lambda h, dk=dk, pg=pg, t0=t0: h.matmul(
                    out=pg[:, 0:TC], lhsT=fwba[:, (8 + dk) * P:(9 + dk) * P],
                    rhs=hn[:, dk * T + t0: dk * T + t0 + TC],
                    start=(dk == 0), stop=(dk == 7)),
                    inc="mm" if dk == 7 else None, waits=c.wc(condg) if dk == 0 else ())
            mmv = b.val("mm")
            b.op("vector", lambda h, pa=pa: h.tensor_scalar(
                out=fgaf[:, 0:TC], in0=pa[:, 0:TC], scalar1=fb1c[:, 0:1],
                scalar2=None, op0=ALU.add), inc="vv", waits=[("mm", mmv)])
            av = b.val("vv")
            c.bank_release(ia, "vv", av)
            b.op("vector", lambda h, pg=pg: h.tensor_scalar(
                out=fgaf[:, TC:2 * TC], in0=pg[:, 0:TC], scalar1=fb1c[:, 1:2],
                scalar2=None, op0=ALU.add), inc="vv", waits=[("mm", mmv)])
            gv = b.val("vv")
            c.bank_release(ig, "vv", gv)
            b.op("scalar", lambda h: h.activation(
                out=fgaf[:, TC:2 * TC], in_=fgaf[:, TC:2 * TC], func=AF.Gelu),
                inc="ss", waits=[("vv", gv)])
            sv = b.val("ss")
            vv = b.op("vector", lambda h: h.tensor_mul(
                out=fhp[:, 0:TC], in0=fgaf[:, 0:TC], in1=fgaf[:, TC:2 * TC]),
                inc="vv", waits=[("ss", sv)])
            b.dma("gpsimd", io["hp_st"][otp * P:(otp + 1) * P, t0:t0 + TC],
                  fhp[:, 0:TC], inc="dG", waits=[("vv", vv)])
            b.barrier()
    c.pop()

    c.push()
    w2b = c.sbuf("fw2b", [P, HIDT * D], BF16)
    w2f = c.sbuf("fw2f", [P, D], F32)
    hpc = c.sbuf("fhpc", [P, HIDT * TC], BF16)
    xic = c.sbuf("fxic", [P, 8 * TC], F32)
    b2c = c.sbuf("fb2c", [P, 8], F32)
    for k in range(8):
        b.dma("sync", b2c[:, k:k + 1], io[f"ff_b2{l}"][k * P:(k + 1) * P, 0:1])
    for kt in range(HIDT):
        b.dma("sync", w2f[:], io[f"ff_w2{l}"][kt * P:(kt + 1) * P, 0:D])
        dv = b.val("dL")
        b.op("vector", lambda h, kt=kt: h.tensor_copy(
            out=w2b[:, kt * D:(kt + 1) * D], in_=w2f[:]), inc="vv",
            waits=[("dL", dv)])
        b.barrier()
    for tc_i in range(NTC):
        t0 = tc_i * TC
        for kt in range(HIDT):
            b.dma("sync", hpc[:, kt * TC:(kt + 1) * TC],
                  io["hp_st"][kt * P:(kt + 1) * P, t0:t0 + TC])
        for k in range(8):
            b.dma("sync", xic[:, k * TC:(k + 1) * TC],
                  io["xi_st"][k * P:(k + 1) * P, t0:t0 + TC])
        dv = b.val("dL")
        b.op("tensor", lambda h: None, waits=[("dL", dv)])
        for ot in range(8):
            i, cond = c.bank()
            pt = c.ps[i]
            for kt in range(HIDT):
                b.op("tensor", lambda h, kt=kt, ot=ot, pt=pt: h.matmul(
                    out=pt[:, 0:TC], lhsT=w2b[:, kt * D + ot * P: kt * D + (ot + 1) * P],
                    rhs=hpc[:, kt * TC:(kt + 1) * TC],
                    start=(kt == 0), stop=(kt == HIDT - 1)),
                    inc="mm" if kt == HIDT - 1 else None,
                    waits=c.wc(cond) if kt == 0 else ())
            mmv = b.val("mm")
            xk = xic[:, ot * TC:(ot + 1) * TC]
            b.op("vector", lambda h, xk=xk, pt=pt: h.tensor_add(
                out=xk, in0=xk, in1=pt[:, 0:TC]), inc="vv", waits=[("mm", mmv)])
            vv = b.op("vector", lambda h, xk=xk, ot=ot: h.tensor_scalar(
                out=xk, in0=xk, scalar1=b2c[:, ot:ot + 1], scalar2=None,
                op0=ALU.add), inc="vv")
            c.bank_release(i, "vv", vv)
            b.dma("gpsimd", io["xi_st"][ot * P:(ot + 1) * P, t0:t0 + TC], xk,
                  inc="dG", waits=[("vv", vv)])
        b.barrier()
    c.pop()



def stage3(c):
    b, io = c.b, c.io
    CLC = 64
    c.push()
    xnst = io["xi_st"]
    xc = c.sbuf("bxc", [P, 8 * TC], F32)
    xb = c.sbuf("bxb", [P, 8 * TC], BF16)
    xq = c.sbuf("bxq", [P, 8 * TC], BF16)
    gamc = c.sbuf("bgam", [P, 8], F32)
    betc = c.sbuf("bbet", [P, 8], F32)
    for k in range(8):
        b.dma("sync", gamc[:, k:k + 1], io["ab_lgam"][k * P:(k + 1) * P, 0:1])
        b.dma("sync", betc[:, k:k + 1], io["ab_lbet"][k * P:(k + 1) * P, 0:1])
    b.barrier()
    for tc_i in range(NTC):
        t0 = tc_i * TC
        for k in range(8):
            b.dma("sync", xc[:, k * TC:(k + 1) * TC],
                  io["xres"][k * P:(k + 1) * P, t0:t0 + TC])
        dv = b.val("dL")
        for k in range(8):
            b.op("vector", lambda h, k=k: h.tensor_copy(
                out=xb[:, k * TC:(k + 1) * TC], in_=xc[:, k * TC:(k + 1) * TC]),
                inc="vv", waits=[("dL", dv)] if k == 0 else ())
            b.op("scalar", lambda h, k=k: h.activation(
                out=xq[:, k * TC:(k + 1) * TC], in_=xc[:, k * TC:(k + 1) * TC],
                func=AF.Square), inc="ss", waits=[("dL", dv)] if k == 0 else ())
        vv, sv = b.val("vv"), b.val("ss")
        b.op("tensor", lambda h: None, waits=[("vv", vv), ("ss", sv)])
        ln_stats_from_tiles(c, [xb[:, k * TC:(k + 1) * TC] for k in range(8)],
                            [xq[:, k * TC:(k + 1) * TC] for k in range(8)])
        ln_rows_math(c, C_INVD, fused_n1=True)
        rv = b.val("vv")
        b.dma("gpsimd", io["rows3_st"][0:1, t0:t0 + TC],
              c.sb["row_m"][0:1, 0:TC], inc="dG", waits=[("vv", rv)])
        b.dma("gpsimd", io["rows3_st"][1:2, t0:t0 + TC],
              c.sb["row_s1"][0:1, 0:TC], inc="dG")
        ir, r_mm = bcast_row(c, c.sb["rowr"][0:1, 0:TC], [("vv", rv)])
        im, m_mm = bcast_row(c, c.sb["rowmr"][0:1, 0:TC], [("vv", rv)])
        lastv = None
        for k in range(8):
            xck = xc[:, k * TC:(k + 1) * TC]
            b.op("vector", lambda h, xck=xck, ir=ir: h.tensor_mul(
                out=xck, in0=xck, in1=c.ps[ir][:, 0:TC]),
                inc="vv", waits=[("mm", m_mm)] if k == 0 else ())
            b.op("vector", lambda h, xck=xck, im=im: h.tensor_add(
                out=xck, in0=xck, in1=c.ps[im][:, 0:TC]), inc="vv")
            lastv = b.op("vector", lambda h, xck=xck, k=k: h.tensor_scalar(
                out=xck, in0=xck, scalar1=gamc[:, k:k + 1], scalar2=betc[:, k:k + 1],
                op0=ALU.mult, op1=ALU.add), inc="vv")
            b.dma("gpsimd", xnst[k * P:(k + 1) * P, t0:t0 + TC], xck,
                  inc="dG", waits=[("vv", lastv)])
        c.bank_release(ir, "vv", lastv)
        c.bank_release(im, "vv", lastv)
        b.barrier()
    c.pop()

    c.push()
    alog = c.sbuf("balog", [P, 8], F32)
    lna = c.sbuf("blna", [P, 8], F32)
    aC = c.sbuf("baC", [P, 8], F32)
    QF8 = c.sbuf("bQF8", [P, 8 * 512], BF16)
    QF1 = c.sbuf("bQF1", [P, 8], F32)
    KF8 = c.sbuf("bKF8", [P, 8 * 512], BF16)
    KHT = c.sbuf("bKHT", [P, 16 * 64], BF16)
    it64 = c.sbuf("bit64", [P, 512], F32)
    ri64 = c.sbuf("bri64", [P, 64], F32)
    ut64 = c.sbuf("but64", [64, 512], F32)
    S_all = c.sbuf("bS", [P, 8 * 64], F32)
    S_bf = c.sbuf("bSbf", [P, 8 * 64], BF16)
    tmpa = c.sbuf("btmpa", [P, 64], F32)
    wq = c.sbuf("bwq", [P, 8 * D], BF16)
    wk = c.sbuf("bwk", [P, 8 * D], BF16)
    wv = c.sbuf("bwv", [P, 8 * D], BF16)
    wg = c.sbuf("bwg", [P, 8 * D], BF16)
    wo_ = c.sbuf("bwo", [P, 8 * D], BF16)
    bq_c = c.sbuf("bbq", [P, 8], F32)
    bk_c = c.sbuf("bbk", [P, 8], F32)
    bg_c = c.sbuf("bbg", [P, 8], F32)
    bo_c = c.sbuf("bbo", [P, 8], F32)
    bvr = c.sbuf("bbvr", [1, D], BF16)
    bkr = c.sbuf("bbkr", [1, D], BF16)
    onesb_r = c.sbuf("bonesbr", [1, P], BF16)
    xnb = c.sbuf("bxnb", [P, 8 * TC], BF16)
    qfm = c.sbuf("bqfm", [P, 8 * TC], BF16)
    kfm = c.sbuf("bkfm", [P, 8 * TC], BF16)
    gfm = c.sbuf("bgfm", [P, 8 * TC], BF16)
    ktm = c.sbuf("bktm", [P, 4 * D], BF16)
    vtm = c.sbuf("bvtm", [P, 4 * D], BF16)
    bxc2 = c.sbuf("bxc2", [P, 8 * TC], F32)
    xad3 = c.sbuf("bxad3", [P, TC], F32)
    qt_ = c.sbuf("bqt", [P, 512], F32)
    kt_ = c.sbuf("bkt", [P, 512], F32)
    ktmh = c.sbuf("bktmh", [P, 512], BF16)
    qtb = c.sbuf("bqtb", [P, 512], BF16)
    ktb = c.sbuf("bktb", [P, 512], BF16)
    qbt = c.sbuf("bqbt", [P, 512], BF16)
    sb_ = c.sbuf("bsb", [P, 512], BF16)

    b.dma("sync", it64[:], io["c_i64t"][:])
    b.dma("sync", ri64[:], io["c_ri64"][:])
    b.dma("sync", ut64[:], io["c_ut64"][:])
    for kt in range(8):
        b.dma("sync", alog[:, kt:kt + 1], io["ab_alog"][kt * P:(kt + 1) * P, 0:1])
    b.barrier()
    fsc = bxc2[:, 0:512]
    for kt in range(8):
        ac = alog[:, kt:kt + 1]
        lc = lna[:, kt:kt + 1]
        b.sop(lambda h, ac=ac: h.activation(out=ac, in_=ac, func=AF.Sigmoid))
        b.sop(lambda h, ac=ac, lc=lc: h.activation(out=lc, in_=ac, func=AF.Ln))
        b.vop(lambda h, kt=kt, ac=ac: h.tensor_copy(out=QF1[:, kt:kt + 1], in_=ac),
              waits=[("ss", b.val("ss"))])
        b.vop(lambda h, lc=lc, fsc=fsc: h.tensor_scalar(
            out=fsc, in0=it64[:], scalar1=lc, scalar2=None, op0=ALU.mult))
        b.sop(lambda h, kt=kt, fsc=fsc: h.activation(
            out=KF8[:, kt * 512:(kt + 1) * 512], in_=fsc, func=AF.Exp,
            scale=c.sb["ccols"][:, C_NEG1:C_NEG1 + 1]), waits=[("vv", b.val("vv"))])
        ssg = b.sop(lambda h, kt=kt, fsc=fsc: h.activation(
            out=QF8[:, kt * 512:(kt + 1) * 512], in_=fsc, func=AF.Exp))
        b.vop(lambda h, kt=kt, lc=lc: h.tensor_scalar(
            out=aC[:, kt:kt + 1], in0=lc, scalar1=c.sb["ccols"][:, C_EIGHT:C_EIGHT + 1],
            scalar2=c.sb["ccols"][:, C_EIGHT:C_EIGHT + 1], op0=ALU.mult, op1=ALU.mult),
            waits=[("ss", ssg)])
        b.sop(lambda h, kt=kt: h.activation(
            out=aC[:, kt:kt + 1], in_=aC[:, kt:kt + 1], func=AF.Exp),
            waits=[("vv", b.val("vv"))])
        b.vop(lambda h, kt=kt, lc=lc: h.tensor_scalar(
            out=tmpa[:], in0=ri64[:], scalar1=lc, scalar2=None, op0=ALU.mult),
            waits=[("ss", b.val("ss"))])
        b.sop(lambda h: h.activation(out=tmpa[:], in_=tmpa[:], func=AF.Exp),
              waits=[("vv", b.val("vv"))])
        svx = b.val("ss")
        for hh2 in range(2):
            i0, cd0 = c.bank()
            mm0 = b.op("tensor", lambda h, i0=i0, hh2=hh2: h.transpose(
                out=c.ps[i0][0:64, 0:64], in_=tmpa[hh2 * 64:(hh2 + 1) * 64, 0:64],
                identity=c.sb["eye2"][hh2 * 64:(hh2 + 1) * 64, hh2 * 64:(hh2 + 1) * 64]),
                inc="mm",
                waits=[("ss", svx)] + c.wc(cd0))
            hd = kt * 2 + hh2
            b.op("vector", lambda h, i0=i0, hd=hd: h.tensor_copy(
                out=KHT[0:64, hd * 64:(hd + 1) * 64], in_=c.ps[i0][0:64, 0:64]),
                inc="vv", waits=[("mm", mm0)])
            vv = b.op("vector", lambda h, i0=i0, hd=hd: h.tensor_copy(
                out=KHT[64:128, hd * 64:(hd + 1) * 64], in_=c.ps[i0][0:64, 0:64]),
                inc="vv")
            c.bank_release(i0, "vv", vv)
        b.barrier()
    b.op("vector", lambda h: h.memset(S_all[:], 0.0), inc="vv")
    b.op("vector", lambda h: h.memset(S_bf[:], 0.0), inc="vv")
    b.barrier()

    for nm, dst in (("ab_WQt", wq), ("ab_WKt", wk), ("ab_WVt", wv),
                    ("ab_Wgt", wg), ("ab_WOt", wo_)):
        for dk in range(8):
            sl = bxc2[:, (dk % 4) * D:(dk % 4 + 1) * D]
            b.dma("sync", sl, io[nm][dk * P:(dk + 1) * P, 0:D])
            dv = b.val("dL")
            b.op("vector", lambda h, dst=dst, dk=dk, sl=sl: h.tensor_copy(
                out=dst[:, dk * D:(dk + 1) * D], in_=sl), inc="vv",
                waits=[("dL", dv)])
            b.barrier()
        b.barrier()
    for k in range(8):
        b.dma("sync", bq_c[:, k:k + 1], io["ab_bQ"][k * P:(k + 1) * P, 0:1])
        b.dma("sync", bk_c[:, k:k + 1], io["ab_bK"][k * P:(k + 1) * P, 0:1])
        b.dma("sync", bg_c[:, k:k + 1], io["ab_bg"][k * P:(k + 1) * P, 0:1])
        b.dma("sync", bo_c[:, k:k + 1], io["ab_bO"][k * P:(k + 1) * P, 0:1])
    b.dma("sync", bxc2[0:1, 0:D], io["ab_bVr"][:])
    b.barrier()
    b.vop(lambda h: h.tensor_copy(out=bvr[:], in_=bxc2[0:1, 0:D]))
    b.dma("sync", bxc2[0:1, 0:D], io["ab_bKr"][:], waits=[("vv", b.val("vv"))])
    b.barrier()
    b.vop(lambda h: h.tensor_copy(out=bkr[:], in_=bxc2[0:1, 0:D]))
    b.vop(lambda h: h.tensor_copy(out=onesb_r[:], in_=c.sb["ones_row"][0:1, 0:P]))
    b.barrier()

    for tc_i in range(NTC):
        t0 = tc_i * TC
        for k in range(8):
            b.dma("sync", bxc2[:, k * TC:(k + 1) * TC],
                  xnst[k * P:(k + 1) * P, t0:t0 + TC])
        dv = b.val("dL")
        for k in range(8):
            b.op("vector", lambda h, k=k: h.tensor_copy(
                out=xnb[:, k * TC:(k + 1) * TC], in_=bxc2[:, k * TC:(k + 1) * TC]),
                inc="vv", waits=[("dL", dv)] if k == 0 else ())
        xv = b.val("vv")
        b.op("tensor", lambda h: None, waits=[("vv", xv)])
        for proj, wmat, bias, dstf in (("q", wq, bq_c, qfm), ("k", wk, bk_c, kfm),
                                       ("g", wg, bg_c, gfm)):
            for ot in range(8):
                i, cond = c.bank()
                pt = c.ps[i]
                for dk in range(8):
                    b.op("tensor", lambda h, dk=dk, ot=ot, pt=pt, wmat=wmat: h.matmul(
                        out=pt[:, 0:TC], lhsT=wmat[:, dk * D + ot * P: dk * D + (ot + 1) * P],
                        rhs=xnb[:, dk * TC:(dk + 1) * TC], start=(dk == 0), stop=(dk == 7)),
                        inc="mm" if dk == 7 else None, waits=c.wc(cond) if dk == 0 else ())
                mmv = b.val("mm")
                vv = b.op("vector", lambda h, pt=pt, ot=ot, dstf=dstf, bias=bias:
                          h.tensor_scalar(
                              out=dstf[:, ot * TC:(ot + 1) * TC], in0=pt[:, 0:TC],
                              scalar1=bias[:, ot:ot + 1], scalar2=None, op0=ALU.add),
                          inc="vv", waits=[("mm", mmv)])
                c.bank_release(i, "vv", vv)
        for ot in range(8):
            b.op("scalar", lambda h, ot=ot: h.activation(
                out=gfm[:, ot * TC:(ot + 1) * TC], in_=gfm[:, ot * TC:(ot + 1) * TC],
                func=AF.Silu), inc="ss", waits=[("vv", b.val("vv"))] if ot == 0 else ())
        for which, wmat, btr, dstt in (("k", wk, bkr, ktm), ("v", wv, bvr, vtm)):
            for sub in range(4):
                for half in range(2):
                    i, cond = c.bank()
                    pt = c.ps[i]
                    for dk in range(8):
                        b.op("tensor", lambda h, dk=dk, sub=sub, half=half, pt=pt,
                             wmat=wmat: h.matmul(
                            out=pt[:, 0:TC],
                            lhsT=xnb[:, dk * TC + sub * P: dk * TC + (sub + 1) * P],
                            rhs=wmat[:, dk * D + half * TC: dk * D + half * TC + TC],
                            start=(dk == 0), stop=False),
                            inc=None, waits=c.wc(cond) if dk == 0 else ())
                    b.op("tensor", lambda h, pt=pt, btr=btr, half=half: h.matmul(
                        out=pt[:, 0:TC], lhsT=onesb_r[0:1, 0:P],
                        rhs=btr[0:1, half * TC: half * TC + TC],
                        start=False, stop=True), inc="mm")
                    mmv = b.val("mm")
                    vv = b.op("vector", lambda h, pt=pt, sub=sub, half=half,
                              dstt=dstt: h.tensor_copy(
                        out=dstt[:, sub * D + half * TC: sub * D + half * TC + TC],
                        in_=pt[:, 0:TC]), inc="vv", waits=[("mm", mmv)])
                    c.bank_release(i, "vv", vv)
        b.barrier()

        for hd in range(16):
            kt2, hp = hd // 2, 64 * (hd % 2)
            qsl = qfm[hp:hp + 64, kt2 * TC:(kt2 + 1) * TC]
            ksl = kfm[hp:hp + 64, (hd // 2) * TC:(hd // 2 + 1) * TC]
            b.op("vector", lambda h, qsl=qsl, kt2=kt2, hp=hp: h.tensor_mul(
                out=qt_[hp:hp + 64, 0:512], in0=qsl,
                in1=QF8[hp:hp + 64, kt2 * 512:(kt2 + 1) * 512]), inc="vv",
                chain=True)
            b.op("vector", lambda h, ksl=ksl, kt2=kt2, hp=hp: h.tensor_mul(
                out=kt_[hp:hp + 64, 0:512], in0=ksl,
                in1=KF8[hp:hp + 64, kt2 * 512:(kt2 + 1) * 512]), inc="vv",
                chain=True)
            b.op("vector", lambda h, hp=hp: h.tensor_copy(
                out=qtb[hp:hp + 64, 0:512], in_=qt_[hp:hp + 64, 0:512]), inc="vv",
                chain=True)
            b.op("vector", lambda h, hp=hp: h.tensor_copy(
                out=ktb[hp:hp + 64, 0:512], in_=kt_[hp:hp + 64, 0:512]), inc="vv",
                chain=True)
            b.op("vector", lambda h, kt2=kt2, hp=hp: h.tensor_scalar(
                out=qbt[hp:hp + 64, 0:512], in0=qt_[hp:hp + 64, 0:512],
                scalar1=QF1[hp:hp + 64, kt2:kt2 + 1], scalar2=None, op0=ALU.mult),
                inc="vv", chain=True)
            for ch in range(8):
                sub, rh = ch // 2, 64 * (ch % 2)
                b.op("vector", lambda h, sub=sub, rh=rh, hd=hd: h.tensor_mul(
                    out=ktmh[rh:rh + 64, sub * 128:sub * 128 + 64],
                    in0=ktm[rh:rh + 64, sub * D + hd * 64: sub * D + hd * 64 + 64],
                    in1=KHT[rh:rh + 64, hd * 64:(hd + 1) * 64]), inc="vv",
                    chain=True)
            b.barrier()
            i1, cd1 = c.bank()
            p1 = c.ps[i1]
            for ch in range(8):
                b.op("tensor", lambda h, ch=ch, p1=p1, hp=hp: h.matmul(
                    out=p1[0:64, ch * 64:(ch + 1) * 64],
                    lhsT=ktb[hp:hp + 64, ch * 64:(ch + 1) * 64],
                    rhs=qtb[hp:hp + 64, ch * 64:(ch + 1) * 64], start=True, stop=True),
                    inc="mm" if ch == 7 else None, waits=c.wc(cd1) if ch == 0 else ())
            mmv = b.val("mm")
            b.op("vector", lambda h, p1=p1: h.tensor_mul(
                out=sb_[0:64, 0:512], in0=p1[0:64, 0:512], in1=ut64[0:64, 0:512]),
                inc="vv", waits=[("mm", mmv)], chain=True)
            vv = b.op("vector", lambda h: h.tensor_copy(
                out=sb_[64:128, 0:512], in_=sb_[0:64, 0:512]), inc="vv",
                chain=True)
            c.bank_release(i1, "vv", vv)
            sbv = vv
            b.barrier()
            i2, cd2 = c.bank()
            p2 = c.ps[i2]
            i2b, cd2b = c.bank()
            p2b = c.ps[i2b]
            ssl = S_all[hp:hp + 64, (hd // 2) * 64:(hd // 2 + 1) * 64]
            sbsl = S_bf[hp:hp + 64, (hd // 2) * 64:(hd // 2 + 1) * 64]
            b.op("tensor", lambda h: None, waits=c.wc(cd2) + c.wc(cd2b))
            sprev_v = b.val("vv")
            for ch in range(8):
                sub, rh = ch // 2, 64 * (ch % 2)
                b.op("tensor", lambda h, ch=ch, p2=p2, sub=sub, rh=rh, hd=hd, hp=hp:
                     h.matmul(
                    out=p2[hp:hp + 64, ch * 64:(ch + 1) * 64],
                    lhsT=vtm[rh:rh + 64, sub * D + hd * 64: sub * D + hd * 64 + 64],
                    rhs=sb_[rh:rh + 64, ch * 64:(ch + 1) * 64], start=True, stop=True),
                    waits=[("vv", sbv if ch == 0 else sprev_v)])
                b.op("tensor", lambda h, ch=ch, p2b=p2b, sbsl=sbsl, hp=hp: h.matmul(
                    out=p2b[hp:hp + 64, ch * 64:(ch + 1) * 64], lhsT=sbsl,
                    rhs=qbt[hp:hp + 64, ch * 64:(ch + 1) * 64], start=True, stop=True),
                    inc="mm")
                mmo = b.val("mm")
                i3, cd3 = c.bank()
                p3 = c.ps[i3]
                mm3 = b.op("tensor", lambda h, ch=ch, p3=p3, sub=sub, rh=rh, hd=hd,
                           hp=hp: h.matmul(
                    out=p3[hp:hp + 64, 0:64],
                    lhsT=ktmh[rh:rh + 64, sub * 128:sub * 128 + 64],
                    rhs=vtm[rh:rh + 64, sub * D + hd * 64: sub * D + hd * 64 + 64],
                    start=True, stop=True), inc="mm", waits=c.wc(cd3))
                b.op("vector", lambda h, p3=p3, ssl=ssl, kt2=kt2, hp=hp:
                     h.scalar_tensor_tensor(
                    out=ssl, in0=ssl, scalar=aC[hp:hp + 64, kt2:kt2 + 1],
                    in1=p3[hp:hp + 64, 0:64], op0=ALU.mult, op1=ALU.add), inc="vv",
                    waits=[("mm", mm3)], chain=True)
                sprev_v = b.op("vector", lambda h, ssl=ssl, sbsl=sbsl: h.tensor_copy(
                    out=sbsl, in_=ssl), inc="vv", chain=True)
                c.bank_release(i3, "vv", sprev_v)
            mmv = b.val("mm")
            b.op("vector", lambda h, p2b=p2b, hp=hp: h.tensor_copy(
                out=kt_[hp:hp + 64, 0:512], in_=p2b[hp:hp + 64, 0:512]),
                inc="vv", waits=[("mm", mmv)], chain=True)
            b.op("vector", lambda h, p2=p2, hd=hd, hp=hp, kt2=kt2: h.tensor_add(
                out=xnb[hp:hp + 64, kt2 * TC:(kt2 + 1) * TC],
                in0=p2[hp:hp + 64, 0:512], in1=kt_[hp:hp + 64, 0:512]),
                inc="vv", chain=True)
            vv = b.op("vector", lambda h, hd=hd, hp=hp, kt2=kt2: h.tensor_mul(
                out=xnb[hp:hp + 64, kt2 * TC:(kt2 + 1) * TC],
                in0=xnb[hp:hp + 64, kt2 * TC:(kt2 + 1) * TC],
                in1=gfm[hp:hp + 64, kt2 * TC:(kt2 + 1) * TC]), inc="vv",
                chain=True)
            c.bank_release(i2, "vv", vv)
            c.bank_release(i2b, "vv", vv)
            b.barrier()
        b.barrier()
        ov = b.val("vv")
        b.op("tensor", lambda h: None, waits=[("vv", ov)])
        for k in range(8):
            b.dma("sync", bxc2[:, k * TC:(k + 1) * TC],
                  io["xres"][k * P:(k + 1) * P, t0:t0 + TC])
        dvx = b.val("dL")
        b.op("tensor", lambda h: None, waits=[("dL", dvx)])
        rv = rows_from_mv(c, io["rows3_st"], t0)
        ir, r_mm = bcast_row(c, c.sb["rowr"][0:1, 0:TC], [("vv", rv)])
        im, m_mm = bcast_row(c, c.sb["rowmr"][0:1, 0:TC], [("vv", rv)])
        lastv = None
        for ot in range(8):
            i, cond = c.bank()
            pt = c.ps[i]
            for dk in range(8):
                b.op("tensor", lambda h, dk=dk, ot=ot, pt=pt: h.matmul(
                    out=pt[:, 0:TC], lhsT=wo_[:, dk * D + ot * P: dk * D + (ot + 1) * P],
                    rhs=xnb[:, dk * TC:(dk + 1) * TC], start=(dk == 0), stop=(dk == 7)),
                    inc="mm" if dk == 7 else None,
                    waits=([("mm", m_mm)] + c.wc(cond)) if dk == 0 else ())
            mmv = b.val("mm")
            xck = bxc2[:, ot * TC:(ot + 1) * TC]
            b.op("vector", lambda h, xck=xck, ir=ir: h.tensor_mul(
                out=xad3[:], in0=xck, in1=c.ps[ir][:, 0:TC]), inc="vv",
                waits=[("mm", mmv)])
            b.op("vector", lambda h, im=im: h.tensor_add(
                out=xad3[:], in0=xad3[:], in1=c.ps[im][:, 0:TC]), inc="vv")
            b.op("vector", lambda h, xck=xck, pt=pt: h.tensor_add(
                out=xck, in0=xck, in1=pt[:, 0:TC]), inc="vv")
            b.op("vector", lambda h, xck=xck, ot=ot: h.tensor_scalar(
                out=xck, in0=xck, scalar1=bo_c[:, ot:ot + 1], scalar2=None,
                op0=ALU.add), inc="vv")
            vv = b.op("vector", lambda h, xck=xck: h.tensor_add(
                out=xck, in0=xck, in1=xad3[:]), inc="vv")
            lastv = vv
            c.bank_release(i, "vv", vv)
            b.dma("gpsimd", io["xres"][ot * P:(ot + 1) * P, t0:t0 + TC], xck,
                  inc="dG", waits=[("vv", vv)])
        c.bank_release(ir, "vv", lastv)
        c.bank_release(im, "vv", lastv)
        b.barrier()
    c.pop()




def wstream(c, src, nrt, ncols, dst, stg, cw=1024):
    b = c.b
    jobs = [(k, c0) for k in range(nrt) for c0 in range(0, ncols, cw)]
    for i, (k, c0) in enumerate(jobs):
        w = min(cw, ncols - c0)
        sl = stg[:, (i % 2) * cw:(i % 2) * cw + w]
        b.dma("sync", sl, src[k * P:(k + 1) * P, c0:c0 + w])
        dv = b.val("dL")
        b.op("vector", lambda h, sl=sl, k=k, c0=c0, w=w: h.tensor_copy(
            out=dst[:, k * ncols + c0:k * ncols + c0 + w], in_=sl), inc="vv",
            waits=[("dL", dv)])
        if i % 2 == 1:
            b.barrier()
    b.barrier()


def stage4(c):
    b, io = c.b, c.io
    ln_driver_sweep(c, io["xres"], io["xi_st"])

    c.push()
    win = c.sbuf("mwin", [P, 8 * 4096], BF16)
    stg = c.sbuf("mstg", [P, 2048], F32)
    wstream(c, io["ma_Win"], 8, 4096, win, stg)
    xnb = c.sbuf("mxnb", [P, 8 * TC], BF16)
    xcf = c.sbuf("mxcf", [P, 8 * TC], F32)
    tev = c.sbuf("mtev", [P, 2 * TC], F32)
    ub = c.sbuf("mub", [P, 2 * TC], BF16)
    for tc_i in range(NTC):
        t0 = tc_i * TC
        for k in range(8):
            b.dma("sync", xcf[:, k * TC:(k + 1) * TC],
                  io["xi_st"][k * P:(k + 1) * P, t0:t0 + TC])
        dv = b.val("dL")
        for k in range(8):
            b.op("vector", lambda h, k=k: h.tensor_copy(
                out=xnb[:, k * TC:(k + 1) * TC], in_=xcf[:, k * TC:(k + 1) * TC]),
                inc="vv", waits=[("dL", dv)] if k == 0 else ())
        xv = b.val("vv")
        b.op("tensor", lambda h: None, waits=[("vv", xv)])
        svs = [None, None]
        dgs = [None, None]
        for ot in range(32):
            s = ot % 2
            i, cond = c.bank()
            pt = c.ps[i]
            for dk in range(8):
                b.op("tensor", lambda h, dk=dk, ot=ot, pt=pt: h.matmul(
                    out=pt[:, 0:TC],
                    lhsT=win[:, dk * 4096 + ot * P: dk * 4096 + (ot + 1) * P],
                    rhs=xnb[:, dk * TC:(dk + 1) * TC], start=(dk == 0), stop=(dk == 7)),
                    inc="mm" if dk == 7 else None, waits=c.wc(cond) if dk == 0 else ())
            mmv = b.val("mm")
            tsl = tev[:, s * TC:(s + 1) * TC]
            usl = ub[:, s * TC:(s + 1) * TC]
            w_ = [("mm", mmv)]
            if svs[s] is not None:
                w_.append(("ss", svs[s]))
            vv = b.op("vector", lambda h, pt=pt, tsl=tsl: h.tensor_copy(
                out=tsl, in_=pt[:, 0:TC]), inc="vv", waits=w_)
            c.bank_release(i, "vv", vv)
            w2 = [("vv", vv)]
            if dgs[s] is not None:
                w2.append(("dG", dgs[s]))
            svs[s] = b.op("scalar", lambda h, tsl=tsl, usl=usl: h.activation(
                out=usl, in_=tsl, func=AF.Silu), inc="ss", waits=w2)
            dgs[s] = b.dma("gpsimd", io["ur_st"][ot * P:(ot + 1) * P, t0:t0 + TC],
                           usl, inc="dG", waits=[("ss", svs[s])])
        b.barrier()
    c.pop()

    c.push()
    wd = c.sbuf("mwd", [P, 16 * 2048], BF16)
    stg2 = c.sbuf("mstg2", [P, 2048], F32)
    wstream(c, io["ma_Wdelta"], 16, 2048, wd, stg2)
    wbc = c.sbuf("mwbc", [P, 2 * 256], BF16)
    for j, nm in enumerate(("ma_WB", "ma_WC")):
        for k in range(16):
            b.dma("sync", stg2[:, 0:16], io[nm][k * P:(k + 1) * P, 0:16])
            dv = b.val("dL")
            b.op("vector", lambda h, j=j, k=k: h.tensor_copy(
                out=wbc[:, j * 256 + k * 16:j * 256 + (k + 1) * 16],
                in_=stg2[:, 0:16]), inc="vv", waits=[("dL", dv)])
            b.barrier()
    bdel = c.sbuf("mbdel", [P, 16], F32)
    for k in range(16):
        b.dma("sync", bdel[:, k:k + 1], io["ma_bdelta"][k * P:(k + 1) * P, 0:1])
    b.barrier()
    uc = c.sbuf("muc", [P, 16 * TC], BF16)
    t0b = c.sbuf("mt0", [P, 2 * TC], F32)
    t2b = c.sbuf("mt2", [P, 2 * TC], F32)
    t3b = c.sbuf("mt3", [P, 2 * TC], F32)
    dob = c.sbuf("mdo", [P, 2 * TC], F32)
    brow = c.sbuf("mbrow", [16, 2 * TC], F32)
    for tc_i in range(NTC):
        t0 = tc_i * TC
        for k in range(16):
            b.dma("sync", uc[:, k * TC:(k + 1) * TC],
                  io["ur_st"][k * P:(k + 1) * P, t0:t0 + TC])
        dv = b.val("dL")
        b.op("tensor", lambda h: None, waits=[("dL", dv)])
        dgs = [None, None]
        for ot in range(16):
            s = ot % 2
            i, cond = c.bank()
            pt = c.ps[i]
            for dk in range(16):
                b.op("tensor", lambda h, dk=dk, ot=ot, pt=pt: h.matmul(
                    out=pt[:, 0:TC],
                    lhsT=wd[:, dk * 2048 + ot * P: dk * 2048 + (ot + 1) * P],
                    rhs=uc[:, dk * TC:(dk + 1) * TC], start=(dk == 0), stop=(dk == 15)),
                    inc="mm" if dk == 15 else None, waits=c.wc(cond) if dk == 0 else ())
            mmv = b.val("mm")
            sl0 = t0b[:, s * TC:(s + 1) * TC]
            sl2 = t2b[:, s * TC:(s + 1) * TC]
            sl3 = t3b[:, s * TC:(s + 1) * TC]
            slo = dob[:, s * TC:(s + 1) * TC]
            w_ = [("mm", mmv)]
            if dgs[s] is not None:
                w_.append(("dG", dgs[s]))
            vv = b.op("vector", lambda h, pt=pt, sl0=sl0, ot=ot: h.tensor_scalar(
                out=sl0, in0=pt[:, 0:TC], scalar1=bdel[:, ot:ot + 1], scalar2=None,
                op0=ALU.add), inc="vv", waits=w_)
            c.bank_release(i, "vv", vv)
            b.op("scalar", lambda h, sl0=sl0, sl2=sl2: h.activation(
                out=sl2, in_=sl0, func=AF.Abs), inc="ss", waits=[("vv", vv)])
            b.op("scalar", lambda h, sl2=sl2, sl3=sl3: h.activation(
                out=sl3, in_=sl2, func=AF.Exp,
                scale=c.sb["ccols"][:, C_NEG1:C_NEG1 + 1]), inc="ss")
            sv = b.op("scalar", lambda h, sl3=sl3: h.activation(
                out=sl3, in_=sl3, func=AF.Ln,
                bias=c.sb["ccols"][:, C_ONE:C_ONE + 1]), inc="ss")
            b.op("vector", lambda h, sl0=sl0, sl2=sl2: h.tensor_add(
                out=sl2, in0=sl0, in1=sl2), inc="vv", waits=[("ss", sv)])
            vv2 = b.op("vector", lambda h, sl2=sl2, sl3=sl3, slo=slo:
                       h.scalar_tensor_tensor(
                out=slo, in0=sl2, scalar=c.sb["ccols"][:, C_HALF:C_HALF + 1],
                in1=sl3, op0=ALU.mult, op1=ALU.add), inc="vv")
            dgs[s] = b.dma("gpsimd", io["dl_st"][ot * P:(ot + 1) * P, t0:t0 + TC],
                           slo, inc="dG", waits=[("vv", vv2)])
        for j, st in enumerate(("Bc_st", "Cc_st")):
            i, cond = c.bank()
            pt = c.ps[i]
            for dk in range(16):
                b.op("tensor", lambda h, dk=dk, j=j, pt=pt: h.matmul(
                    out=pt[0:16, 0:TC],
                    lhsT=wbc[:, j * 256 + dk * 16:j * 256 + (dk + 1) * 16],
                    rhs=uc[:, dk * TC:(dk + 1) * TC], start=(dk == 0), stop=(dk == 15)),
                    inc="mm" if dk == 15 else None, waits=c.wc(cond) if dk == 0 else ())
            mmv = b.val("mm")
            bsl = brow[0:16, j * TC:(j + 1) * TC]
            vv = b.op("vector", lambda h, pt=pt, bsl=bsl: h.tensor_copy(
                out=bsl, in_=pt[0:16, 0:TC]), inc="vv", waits=[("mm", mmv)])
            c.bank_release(i, "vv", vv)
            b.dma("gpsimd", io[st][0:16, t0:t0 + TC], bsl, inc="dG",
                  waits=[("vv", vv)])
        b.barrier()
    c.pop()

    c.push()
    Bbc = c.sbuf("mBbc", [P, 16 * TC], BF16)
    Cbc = c.sbuf("mCbc", [P, 16 * TC], BF16)
    acol = c.sbuf("macol", [P, 256], F32)
    dcol = c.sbuf("mdcol", [P, 16], F32)
    crs = c.sbuf("mcrs", [P, 256], F32)
    brow2 = c.sbuf("mbrow2", [1, TC], F32)
    dc = c.sbuf("mdc", [P, TC], F32)
    ucn = c.sbuf("mucn", [P, TC], BF16)
    rcn = c.sbuf("mrcn", [P, TC], BF16)
    du = c.sbuf("mdu", [P, TC], F32)
    ya = c.sbuf("mya", [P, TC], F32)
    dA = c.sbuf("mdA", [P, TC], F32)
    bsc = c.sbuf("mbsc", [P, TC], F32)
    hsc = c.sbuf("mhsc", [P, TC], F32)
    tmp2 = c.sbuf("mtmp2", [P, TC], F32)
    ygb = c.sbuf("mygb", [P, TC], BF16)
    for k in range(16):
        b.dma("sync", acol[:, k * 16:(k + 1) * 16],
              io["ma_Alog"][k * P:(k + 1) * P, 0:16])
        b.dma("sync", dcol[:, k:k + 1], io["ma_D"][k * P:(k + 1) * P, 0:1])
    b.barrier()
    b.sop(lambda h: h.activation(out=acol[:], in_=acol[:], func=AF.Exp))
    b.vop(lambda h: h.tensor_single_scalar(
        out=acol[:], in_=acol[:], scalar=c.sb["ccols"][:, C_NEG1:C_NEG1 + 1],
        op=ALU.mult), waits=[("ss", b.val("ss"))])
    b.op("vector", lambda h: h.memset(crs[:], 0.0), inc="vv")
    b.barrier()
    for tc_i in range(NTC):
        t0 = tc_i * TC
        mmprev = None
        for st, dst in (("Bc_st", Bbc), ("Cc_st", Cbc)):
            for n in range(16):
                w_ = [("mm", mmprev)] if mmprev is not None else []
                dvb = b.dma("sync", brow2[:], io[st][n:n + 1, t0:t0 + TC], waits=w_)
                i, cond = c.bank()
                mmprev = b.op("tensor", lambda h, i=i: h.matmul(
                    out=c.ps[i][:, 0:TC], lhsT=c.sb["ones_row"][0:1, 0:P],
                    rhs=brow2[0:1, 0:TC], start=True, stop=True), inc="mm",
                    waits=[("dL", dvb)] + c.wc(cond))
                vv = b.op("vector", lambda h, i=i, dst=dst, n=n: h.tensor_copy(
                    out=dst[:, n * TC:(n + 1) * TC], in_=c.ps[i][:, 0:TC]),
                    inc="vv", waits=[("mm", mmprev)])
                c.bank_release(i, "vv", vv)
        lastsv = None
        lastvv = None
        lastdg = None
        for tl in range(16):
            w_ = []
            if lastsv is not None:
                w_ = [("ss", lastsv), ("vv", lastvv)]
            b.dma("sync", dc[:], io["dl_st"][tl * P:(tl + 1) * P, t0:t0 + TC],
                  waits=w_)
            b.dma("sync", ucn[:], io["ur_st"][tl * P:(tl + 1) * P, t0:t0 + TC])
            b.dma("sync", rcn[:], io["ur_st"][(16 + tl) * P:(17 + tl) * P, t0:t0 + TC])
            dv = b.val("dL")
            b.op("vector", lambda h: h.tensor_mul(out=du[:], in0=dc[:], in1=ucn[:]),
                 inc="vv", waits=[("dL", dv)])
            b.op("vector", lambda h, tl=tl: h.tensor_scalar(
                out=ya[:], in0=ucn[:], scalar1=dcol[:, tl:tl + 1], scalar2=None,
                op0=ALU.mult), inc="vv")
            scan_vv = None
            for n in range(16):
                cidx = tl * 16 + n
                w_ = [("vv", scan_vv)] if scan_vv is not None else [("vv", b.val("vv"))]
                sv = b.op("scalar", lambda h, cidx=cidx: h.activation(
                    out=dA[:], in_=dc[:], func=AF.Exp,
                    scale=acol[:, cidx:cidx + 1]), inc="ss", waits=w_)
                b.op("vector", lambda h, n=n: h.tensor_mul(
                    out=bsc[:], in0=du[:], in1=Bbc[:, n * TC:(n + 1) * TC]), inc="vv")
                scan_vv = b.op("vector", lambda h, cidx=cidx: h.tensor_tensor_scan(
                    out=hsc[:], data0=dA[:], data1=bsc[:],
                    initial=crs[:, cidx:cidx + 1], op0=ALU.mult, op1=ALU.add),
                    inc="vv", waits=[("ss", sv)], chain=True)
                b.op("vector", lambda h, cidx=cidx: h.tensor_copy(
                    out=crs[:, cidx:cidx + 1], in_=hsc[:, TC - 1:TC]), inc="vv",
                    chain=True)
                b.op("vector", lambda h, n=n: h.tensor_mul(
                    out=tmp2[:], in0=hsc[:], in1=Cbc[:, n * TC:(n + 1) * TC]), inc="vv")
                b.op("vector", lambda h: h.tensor_add(
                    out=ya[:], in0=ya[:], in1=tmp2[:]), inc="vv")
            w_ = [("dG", lastdg)] if lastdg is not None else []
            lastvv = b.op("vector", lambda h: h.tensor_mul(
                out=ygb[:], in0=ya[:], in1=rcn[:]), inc="vv", waits=w_)
            lastsv = b.val("ss")
            lastdg = b.dma("gpsimd", io["yg_st"][tl * P:(tl + 1) * P, t0:t0 + TC],
                           ygb[:], inc="dG", waits=[("vv", lastvv)])
        b.barrier()
    c.pop()

    c.push()
    wout = c.sbuf("mwo", [P, 16 * D], BF16)
    stg3 = c.sbuf("mstg3", [P, 2048], F32)
    wstream(c, io["ma_Wout"], 16, D, wout, stg3)
    ygc = c.sbuf("mygc", [P, 16 * TC], BF16)
    xc5 = c.sbuf("mxc5", [P, 8 * TC], F32)
    for tc_i in range(NTC):
        t0 = tc_i * TC
        for k in range(16):
            b.dma("sync", ygc[:, k * TC:(k + 1) * TC],
                  io["yg_st"][k * P:(k + 1) * P, t0:t0 + TC])
        for k in range(8):
            b.dma("sync", xc5[:, k * TC:(k + 1) * TC],
                  io["xres"][k * P:(k + 1) * P, t0:t0 + TC])
        dv = b.val("dL")
        b.op("tensor", lambda h: None, waits=[("dL", dv)])
        for ot in range(8):
            i, cond = c.bank()
            pt = c.ps[i]
            for dk in range(16):
                b.op("tensor", lambda h, dk=dk, ot=ot, pt=pt: h.matmul(
                    out=pt[:, 0:TC],
                    lhsT=wout[:, dk * D + ot * P: dk * D + (ot + 1) * P],
                    rhs=ygc[:, dk * TC:(dk + 1) * TC], start=(dk == 0), stop=(dk == 15)),
                    inc="mm" if dk == 15 else None,
                    waits=([("dL", dv)] + c.wc(cond)) if dk == 0 else ())
            mmv = b.val("mm")
            xck = xc5[:, ot * TC:(ot + 1) * TC]
            vv = b.op("vector", lambda h, xck=xck, pt=pt: h.tensor_add(
                out=xck, in0=xck, in1=pt[:, 0:TC]), inc="vv", waits=[("mm", mmv)])
            c.bank_release(i, "vv", vv)
            b.dma("gpsimd", io["xres"][ot * P:(ot + 1) * P, t0:t0 + TC], xck,
                  inc="dG", waits=[("vv", vv)])
        b.barrier()
    c.pop()




def binarize_load(c, wsrc, nrt, ncols, dst, stg, sgn_st, arow, inv_col):
    b = c.b
    afb = c.sbuf("afb", [P, 1024], F32)
    b.op("vector", lambda h, arow=arow: h.memset(arow[0:1, 0:4096], 0.0), inc="vv")
    jobs = [(k, c0) for k in range(nrt) for c0 in range(0, ncols, 1024)]
    lastmm = None
    for i, (k, c0) in enumerate(jobs):
        sl = stg[:, (i % 2) * 1024:(i % 2) * 1024 + 1024]
        w_ = [("ss", b.val("ss")), ("mm", lastmm)] if lastmm is not None else []
        b.dma("sync", sl, wsrc[k * P:(k + 1) * P, c0:c0 + 1024], waits=w_)
        dv = b.val("dL")
        sv = b.op("scalar", lambda h, sl=sl, k=k, c0=c0: h.activation(
            out=dst[:, k * ncols + c0:k * ncols + c0 + 1024], in_=sl,
            func=AF.Sign), inc="ss", waits=[("dL", dv)])
        sv2 = b.op("scalar", lambda h, sl=sl: h.activation(
            out=afb[:], in_=sl, func=AF.Abs), inc="ss")
        b.dma("gpsimd", sgn_st[k * P:(k + 1) * P, c0:c0 + 1024],
              dst[:, k * ncols + c0:k * ncols + c0 + 1024], inc="dG",
              waits=[("ss", sv)])
        for j in range(2):
            i2, cond = c.bank()
            mmv = b.op("tensor", lambda h, i2=i2, j=j: h.matmul(
                out=c.ps[i2][0:1, 0:512], lhsT=c.sb["ones_f"][:],
                rhs=afb[:, j * 512:(j + 1) * 512], start=True, stop=True),
                inc="mm", waits=[("ss", sv2)] + c.wc(cond))
            vv = b.op("vector", lambda h, i2=i2, j=j, c0=c0, arow=arow:
                      h.tensor_add(
                out=arow[0:1, c0 + j * 512:c0 + (j + 1) * 512],
                in0=arow[0:1, c0 + j * 512:c0 + (j + 1) * 512],
                in1=c.ps[i2][0:1, 0:512]), inc="vv", waits=[("mm", mmv)])
            c.bank_release(i2, "vv", vv)
            lastmm = mmv
    sv = b.op("scalar", lambda h, arow=arow, inv_col=inv_col: h.activation(
        out=arow[0:1, 0:4096], in_=arow[0:1, 0:4096], func=AF.Copy,
        scale=cc1(c, inv_col)), inc="ss", waits=[("vv", b.val("vv"))])
    b.barrier()


def alpha_cols(c, arow, ncols, acols):
    b = c.b
    for ot in range(ncols // P):
        i, cond = c.bank()
        mmv = b.op("tensor", lambda h, i=i, ot=ot, arow=arow: h.matmul(
            out=c.ps[i][:, 0:1], lhsT=arow[0:1, ot * P:(ot + 1) * P],
            rhs=c.sb["ones_row"][0:1, 0:1], start=True, stop=True), inc="mm",
            waits=c.wc(cond))
        vv = b.op("vector", lambda h, i=i, ot=ot, acols=acols: h.tensor_copy(
            out=acols[:, ot:ot + 1], in_=c.ps[i][:, 0:1]), inc="vv",
            waits=[("mm", mmv)])
        c.bank_release(i, "vv", vv)
    b.barrier()


def stage5(c, first):
    b, io = c.b, c.io
    ln_driver_sweep(c, io["xres"], io["xi_st"])

    c.push()
    sgn1 = c.sbuf("fsg1", [P, 8 * 4096], BF16)
    stg = c.sbuf("fstg", [P, 2048], F32)
    arow = c.sbuf("farow", [1, 4096], F32)
    a1c = c.sbuf("fa1c", [P, 32], F32)
    b1c = c.sbuf("fb1c", [P, 32], F32)
    if first:
        binarize_load(c, io["fb_w1T"], 8, 4096, sgn1, stg, io["sg1_st"],
                      arow, C_INVD)
        b.dma("gpsimd", io["al_st"][0:1, 0:4096], arow[0:1, 0:4096], inc="dG",
              waits=[("ss", b.val("ss"))])
        b.barrier()
    else:
        for k in range(8):
            b.dma("sync", sgn1[:, k * 4096:(k + 1) * 4096],
                  io["sg1_st"][k * P:(k + 1) * P, 0:4096])
        b.dma("sync", arow[0:1, 0:4096], io["al_st"][0:1, 0:4096])
        b.barrier()
    alpha_cols(c, arow, 4096, a1c)
    for k in range(32):
        b.dma("sync", b1c[:, k:k + 1], io["fb_b1"][k * P:(k + 1) * P, 0:1])
    b.barrier()
    xnb = c.sbuf("fxnb", [P, 8 * TC], BF16)
    xcf = c.sbuf("fxcf", [P, 8 * TC], F32)
    tev = c.sbuf("ftev", [P, 2 * TC], F32)
    hb = c.sbuf("fhb", [P, 2 * TC], BF16)
    for tc_i in range(NTC):
        t0 = tc_i * TC
        for k in range(8):
            b.dma("sync", xcf[:, k * TC:(k + 1) * TC],
                  io["xi_st"][k * P:(k + 1) * P, t0:t0 + TC])
        dv = b.val("dL")
        for k in range(8):
            b.op("vector", lambda h, k=k: h.tensor_copy(
                out=xnb[:, k * TC:(k + 1) * TC], in_=xcf[:, k * TC:(k + 1) * TC]),
                inc="vv", waits=[("dL", dv)] if k == 0 else ())
        xv = b.val("vv")
        b.op("tensor", lambda h: None, waits=[("vv", xv)])
        svs = [None, None]
        dgs = [None, None]
        for ot in range(32):
            s = ot % 2
            i, cond = c.bank()
            pt = c.ps[i]
            for dk in range(8):
                b.op("tensor", lambda h, dk=dk, ot=ot, pt=pt: h.matmul(
                    out=pt[:, 0:TC],
                    lhsT=sgn1[:, dk * 4096 + ot * P: dk * 4096 + (ot + 1) * P],
                    rhs=xnb[:, dk * TC:(dk + 1) * TC], start=(dk == 0), stop=(dk == 7)),
                    inc="mm" if dk == 7 else None, waits=c.wc(cond) if dk == 0 else ())
            mmv = b.val("mm")
            tsl = tev[:, s * TC:(s + 1) * TC]
            hsl = hb[:, s * TC:(s + 1) * TC]
            w_ = [("mm", mmv)]
            if svs[s] is not None:
                w_.append(("ss", svs[s]))
            vv = b.op("vector", lambda h, pt=pt, tsl=tsl, ot=ot: h.tensor_scalar(
                out=tsl, in0=pt[:, 0:TC], scalar1=a1c[:, ot:ot + 1],
                scalar2=b1c[:, ot:ot + 1], op0=ALU.mult, op1=ALU.add), inc="vv",
                waits=w_)
            c.bank_release(i, "vv", vv)
            w2 = [("vv", vv)]
            if dgs[s] is not None:
                w2.append(("dG", dgs[s]))
            svs[s] = b.op("scalar", lambda h, tsl=tsl, hsl=hsl: h.activation(
                out=hsl, in_=tsl, func=AF.Gelu), inc="ss", waits=w2)
            dgs[s] = b.dma("gpsimd", io["ur_st"][ot * P:(ot + 1) * P, t0:t0 + TC],
                           hsl, inc="dG", waits=[("ss", svs[s])])
        b.barrier()
    c.pop()

    c.push()
    sgn2 = c.sbuf("fsg2", [P, 32 * D], BF16)
    stg2 = c.sbuf("fstg2", [P, 2048], F32)
    arow2 = c.sbuf("farow2", [1, 4096], F32)
    a2c = c.sbuf("fa2c", [P, 8], F32)
    b2c = c.sbuf("fb2c", [P, 8], F32)
    if first:
        binarize_load(c, io["fb_w2T"], 32, D, sgn2, stg2, io["sg2_st"],
                      arow2, C_INV4K)
        b.dma("gpsimd", io["al_st"][1:2, 0:4096], arow2[0:1, 0:4096], inc="dG",
              waits=[("ss", b.val("ss"))])
        b.barrier()
    else:
        for k in range(32):
            b.dma("sync", sgn2[:, k * D:(k + 1) * D],
                  io["sg2_st"][k * P:(k + 1) * P, 0:D])
        b.dma("sync", arow2[0:1, 0:4096], io["al_st"][1:2, 0:4096])
        b.barrier()
    alpha_cols(c, arow2, D, a2c)
    for k in range(8):
        b.dma("sync", b2c[:, k:k + 1], io["fb_b2"][k * P:(k + 1) * P, 0:1])
    b.barrier()
    hbc = c.sbuf("fhbc", [P, 32 * TC], BF16)
    xc6 = c.sbuf("fxc6", [P, 8 * TC], F32)
    tmp6 = c.sbuf("ftmp6", [P, TC], F32)
    for tc_i in range(NTC):
        t0 = tc_i * TC
        for k in range(32):
            b.dma("sync", hbc[:, k * TC:(k + 1) * TC],
                  io["ur_st"][k * P:(k + 1) * P, t0:t0 + TC])
        for k in range(8):
            b.dma("sync", xc6[:, k * TC:(k + 1) * TC],
                  io["xres"][k * P:(k + 1) * P, t0:t0 + TC])
        dv = b.val("dL")
        b.op("tensor", lambda h: None, waits=[("dL", dv)])
        for ot in range(8):
            i, cond = c.bank()
            pt = c.ps[i]
            for dk in range(32):
                b.op("tensor", lambda h, dk=dk, ot=ot, pt=pt: h.matmul(
                    out=pt[:, 0:TC],
                    lhsT=sgn2[:, dk * D + ot * P: dk * D + (ot + 1) * P],
                    rhs=hbc[:, dk * TC:(dk + 1) * TC], start=(dk == 0),
                    stop=(dk == 31)),
                    inc="mm" if dk == 31 else None, waits=c.wc(cond) if dk == 0 else ())
            mmv = b.val("mm")
            xck = xc6[:, ot * TC:(ot + 1) * TC]
            b.op("vector", lambda h, pt=pt, ot=ot: h.tensor_scalar(
                out=tmp6[:], in0=pt[:, 0:TC], scalar1=a2c[:, ot:ot + 1],
                scalar2=b2c[:, ot:ot + 1], op0=ALU.mult, op1=ALU.add), inc="vv",
                waits=[("mm", mmv)])
            vv = b.op("vector", lambda h, xck=xck: h.tensor_add(
                out=xck, in0=xck, in1=tmp6[:]), inc="vv")
            c.bank_release(i, "vv", vv)
            b.dma("gpsimd", io["xres"][ot * P:(ot + 1) * P, t0:t0 + TC], xck,
                  inc="dG", waits=[("vv", vv)])
        b.barrier()
    c.pop()




def stage6(c):
    b, io = c.b, c.io
    if E88_SKIP_A:
        stage6b(c)
        return
    ln_driver_sweep(c, io["xres"], io["xi_st"])

    c.push()
    wqt = c.sbuf("ewq", [P, 8 * 256], BF16)
    wkt = c.sbuf("ewk", [P, 8 * 256], BF16)
    wvt = c.sbuf("ewv", [P, 8 * 512], BF16)
    wgt = c.sbuf("ewg", [P, 8 * 512], BF16)
    wat = c.sbuf("ewa", [P, 8 * 8], BF16)
    estg = c.sbuf("estg", [P, 2048], F32)
    wstream(c, io["eb_WqT"], 8, 256, wqt, estg, cw=256)
    wstream(c, io["eb_WkT"], 8, 256, wkt, estg, cw=256)
    wstream(c, io["eb_WvT"], 8, 512, wvt, estg, cw=512)
    wstream(c, io["eb_WgT"], 8, 512, wgt, estg, cw=512)
    wstream(c, io["eb_WaT"], 8, 8, wat, estg, cw=8)
    qwc = c.sbuf("eqwc", [P, 8], F32)
    kwc = c.sbuf("ekwc", [P, 8], F32)
    vwc = c.sbuf("evwc", [P, 16], F32)
    for tl in range(2):
        b.dma("sync", qwc[:, tl * 4:(tl + 1) * 4], io["eb_qcw"][tl * P:(tl + 1) * P, 0:4])
        b.dma("sync", kwc[:, tl * 4:(tl + 1) * 4], io["eb_kcw"][tl * P:(tl + 1) * P, 0:4])
    for tl in range(4):
        b.dma("sync", vwc[:, tl * 4:(tl + 1) * 4], io["eb_vcw"][tl * P:(tl + 1) * P, 0:4])
    nea = c.sbuf("enea", [8, 1], F32)
    dtb = c.sbuf("edtb", [8, 1], F32)
    bd4c = c.sbuf("ebd4", [4, P], F32)
    bo32c = c.sbuf("ebo32", [P, 4], F32)
    b.dma("sync", nea[:], io["eb_alog"][:])
    b.dma("sync", dtb[:], io["eb_dtb"][:])
    b.dma("sync", bd4c[:], io["c_bd4T"][:])
    b.dma("sync", bo32c[:], io["c_bo32"][:])
    b.barrier()
    b.sop(lambda h: h.activation(out=nea[:], in_=nea[:], func=AF.Exp))
    b.vop(lambda h: h.tensor_single_scalar(
        out=nea[:], in_=nea[:], scalar=c.sb["ccols"][0:8, C_NEG1:C_NEG1 + 1],
        op=ALU.mult), waits=[("ss", b.val("ss"))])
    b.barrier()
    HW_ = TC + 3
    qh = c.sbuf("eqh", [P, 2 * HW_], F32)
    kh = c.sbuf("ekh", [P, 2 * HW_], F32)
    vh = c.sbuf("evh", [P, 4 * HW_], F32)
    b.op("vector", lambda h: h.memset(qh[:], 0.0), inc="vv")
    b.op("vector", lambda h: h.memset(kh[:], 0.0), inc="vv")
    b.op("vector", lambda h: h.memset(vh[:], 0.0), inc="vv")
    b.barrier()
    xnb = c.sbuf("exnb", [P, 8 * TC], BF16)
    xcf = c.sbuf("excf", [P, 8 * TC], F32)
    acc = c.sbuf("eacc", [P, TC], F32)
    sil = c.sbuf("esil", [P, TC], F32)
    sqb = c.sbuf("esqb", [P, TC], BF16)
    rr4 = c.sbuf("err4", [4, TC], F32)
    arw = c.sbuf("earw", [8, TC], F32)
    as1 = c.sbuf("eas1", [8, TC], F32)
    as2 = c.sbuf("eas2", [8, TC], F32)

    def conv_tile(hal, base, wc_, wi, mmv, carry, accw=()):
        if carry:
            b.op("vector", lambda h: h.tensor_copy(
                out=hal[:, 0:3], in_=hal[:, TC:TC + 3]), inc="vv")
        vv = b.op("vector", lambda h: h.tensor_copy(
            out=hal[:, 3:3 + TC], in_=c.ps[base][:, 0:TC]), inc="vv",
            waits=[("mm", mmv)])
        b.op("vector", lambda h: h.tensor_scalar(
            out=acc[:], in0=hal[:, 0:TC], scalar1=wc_[:, wi * 4:wi * 4 + 1],
            scalar2=None, op0=ALU.mult), inc="vv", waits=list(accw))
        for j in range(1, 4):
            vv = b.op("vector", lambda h, j=j: h.scalar_tensor_tensor(
                out=acc[:], in0=hal[:, j:j + TC], scalar=wc_[:, wi * 4 + j:wi * 4 + j + 1],
                in1=acc[:], op0=ALU.mult, op1=ALU.add), inc="vv")
        return vv

    for tc_i in range(NTC):
        t0 = tc_i * TC
        for k in range(8):
            b.dma("sync", xcf[:, k * TC:(k + 1) * TC],
                  io["xi_st"][k * P:(k + 1) * P, t0:t0 + TC])
        dv = b.val("dL")
        for k in range(8):
            b.op("vector", lambda h, k=k: h.tensor_copy(
                out=xnb[:, k * TC:(k + 1) * TC], in_=xcf[:, k * TC:(k + 1) * TC]),
                inc="vv", waits=[("dL", dv)] if k == 0 else ())
        xv = b.val("vv")
        b.op("tensor", lambda h: None, waits=[("vv", xv)])
        acc_sv = None
        for wres, wc_, ncols, hal8, dst_st in (
                (wqt, qwc, 256, qh, "qn_st"), (wkt, kwc, 256, kh, "kn_st")):
            for tl in range(2):
                i, cond = c.bank()
                pt = c.ps[i]
                for dk in range(8):
                    b.op("tensor", lambda h, dk=dk, tl=tl, pt=pt, wres=wres,
                         ncols=ncols: h.matmul(
                        out=pt[:, 0:TC],
                        lhsT=wres[:, dk * ncols + tl * P: dk * ncols + (tl + 1) * P],
                        rhs=xnb[:, dk * TC:(dk + 1) * TC], start=(dk == 0),
                        stop=(dk == 7)),
                        inc="mm" if dk == 7 else None,
                        waits=c.wc(cond) if dk == 0 else ())
                mmv = b.val("mm")
                hal = hal8[:, tl * HW_:(tl + 1) * HW_]
                vv = conv_tile(hal, i, wc_, tl, mmv, tc_i > 0,
                               accw=[("ss", acc_sv)] if acc_sv else ())
                c.bank_release(i, "vv", vv)
                sv = b.op("scalar", lambda h: h.activation(
                    out=sil[:], in_=acc[:], func=AF.Silu), inc="ss",
                    waits=[("vv", vv)])
                sv2 = b.op("scalar", lambda h: h.activation(
                    out=acc[:], in_=sil[:], func=AF.Square), inc="ss")
                acc_sv = sv2
                i2, cond2 = c.bank()
                mmn = b.op("tensor", lambda h, i2=i2: h.matmul(
                    out=c.ps[i2][0:4, 0:TC], lhsT=bo32c[:], rhs=acc[:],
                    start=True, stop=True), inc="mm",
                    waits=[("ss", sv2)] + c.wc(cond2))
                vv = b.op("vector", lambda h, i2=i2: h.tensor_copy(
                    out=rr4[0:4, 0:TC], in_=c.ps[i2][0:4, 0:TC]), inc="vv",
                    waits=[("mm", mmn)])
                c.bank_release(i2, "vv", vv)
                b.op("scalar", lambda h: h.activation(
                    out=rr4[0:4, 0:TC], in_=rr4[0:4, 0:TC], func=AF.Ln), inc="ss",
                    waits=[("vv", vv)])
                b.op("scalar", lambda h: h.activation(
                    out=rr4[0:4, 0:TC], in_=rr4[0:4, 0:TC], func=AF.Exp,
                    scale=c.sb["ccols"][0:4, C_HALF:C_HALF + 1]), inc="ss")
                b.op("scalar", lambda h: h.activation(
                    out=rr4[0:4, 0:TC], in_=rr4[0:4, 0:TC], func=AF.Ln,
                    bias=c.sb["ccols"][0:4, C_EPS6:C_EPS6 + 1]), inc="ss")
                sv3 = b.op("scalar", lambda h: h.activation(
                    out=rr4[0:4, 0:TC], in_=rr4[0:4, 0:TC], func=AF.Exp,
                    scale=c.sb["ccols"][0:4, C_NEG1:C_NEG1 + 1]), inc="ss")
                i3, cond3 = c.bank()
                mmb = b.op("tensor", lambda h, i3=i3: h.matmul(
                    out=c.ps[i3][:, 0:TC], lhsT=bd4c[:], rhs=rr4[0:4, 0:TC],
                    start=True, stop=True), inc="mm",
                    waits=[("ss", sv3)] + c.wc(cond3))
                vv = b.op("vector", lambda h, i3=i3: h.tensor_mul(
                    out=sil[:], in0=sil[:], in1=c.ps[i3][:, 0:TC]), inc="vv",
                    waits=[("mm", mmb)])
                c.bank_release(i3, "vv", vv)
                b.dma("gpsimd", io[dst_st][tl * P:(tl + 1) * P, t0:t0 + TC],
                      sil[:], inc="dG", waits=[("vv", vv)])
                b.op("scalar", lambda h: None, inc="ss",
                     waits=[("dG", b.val("dG"))])
        for tl in range(4):
            i, cond = c.bank()
            pt = c.ps[i]
            for dk in range(8):
                b.op("tensor", lambda h, dk=dk, tl=tl, pt=pt: h.matmul(
                    out=pt[:, 0:TC],
                    lhsT=wvt[:, dk * 512 + tl * P: dk * 512 + (tl + 1) * P],
                    rhs=xnb[:, dk * TC:(dk + 1) * TC], start=(dk == 0), stop=(dk == 7)),
                    inc="mm" if dk == 7 else None, waits=c.wc(cond) if dk == 0 else ())
            mmv = b.val("mm")
            hal = vh[:, tl * HW_:(tl + 1) * HW_]
            vv = conv_tile(hal, i, vwc, tl, mmv, tc_i > 0,
                           accw=[("ss", acc_sv)] if acc_sv else ())
            c.bank_release(i, "vv", vv)
            sv = b.op("scalar", lambda h: h.activation(
                out=sqb[:], in_=acc[:], func=AF.Silu), inc="ss", waits=[("vv", vv)])
            acc_sv = sv
            b.dma("gpsimd", io["v_st"][tl * P:(tl + 1) * P, t0:t0 + TC], sqb[:],
                  inc="dG", waits=[("ss", sv)])
            b.op("scalar", lambda h: None, inc="ss",
                 waits=[("dG", b.val("dG"))])
        for tl in range(4):
            i, cond = c.bank()
            pt = c.ps[i]
            for dk in range(8):
                b.op("tensor", lambda h, dk=dk, tl=tl, pt=pt: h.matmul(
                    out=pt[:, 0:TC],
                    lhsT=wgt[:, dk * 512 + tl * P: dk * 512 + (tl + 1) * P],
                    rhs=xnb[:, dk * TC:(dk + 1) * TC], start=(dk == 0), stop=(dk == 7)),
                    inc="mm" if dk == 7 else None, waits=c.wc(cond) if dk == 0 else ())
            mmv = b.val("mm")
            vv = b.op("vector", lambda h, pt=pt: h.tensor_copy(
                out=acc[:], in_=pt[:, 0:TC]), inc="vv",
                waits=[("mm", mmv)] + ([("ss", acc_sv)] if acc_sv else []))
            c.bank_release(i, "vv", vv)
            sv = b.op("scalar", lambda h: h.activation(
                out=sqb[:], in_=acc[:], func=AF.Sigmoid), inc="ss", waits=[("vv", vv)])
            acc_sv = sv
            b.dma("gpsimd", io["gt_st"][tl * P:(tl + 1) * P, t0:t0 + TC], sqb[:],
                  inc="dG", waits=[("ss", sv)])
            b.op("scalar", lambda h: None, inc="ss",
                 waits=[("dG", b.val("dG"))])
        i, cond = c.bank()
        pt = c.ps[i]
        for dk in range(8):
            b.op("tensor", lambda h, dk=dk, pt=pt: h.matmul(
                out=pt[0:8, 0:TC], lhsT=wat[:, dk * 8:(dk + 1) * 8],
                rhs=xnb[:, dk * TC:(dk + 1) * TC], start=(dk == 0), stop=(dk == 7)),
                inc="mm" if dk == 7 else None, waits=c.wc(cond) if dk == 0 else ())
        mmv = b.val("mm")
        vv = b.op("vector", lambda h, pt=pt: h.tensor_scalar(
            out=arw[:], in0=pt[0:8, 0:TC], scalar1=dtb[:], scalar2=None,
            op0=ALU.add), inc="vv", waits=[("mm", mmv)])
        c.bank_release(i, "vv", vv)
        b.op("scalar", lambda h: h.activation(out=as1[:], in_=arw[:], func=AF.Abs),
             inc="ss", waits=[("vv", vv)])
        b.op("scalar", lambda h: h.activation(
            out=as2[:], in_=as1[:], func=AF.Exp,
            scale=c.sb["ccols"][0:8, C_NEG1:C_NEG1 + 1]), inc="ss")
        sv = b.op("scalar", lambda h: h.activation(
            out=as2[:], in_=as2[:], func=AF.Ln,
            bias=c.sb["ccols"][0:8, C_ONE:C_ONE + 1]), inc="ss")
        b.op("vector", lambda h: h.tensor_add(out=as1[:], in0=arw[:], in1=as1[:]),
             inc="vv", waits=[("ss", sv)])
        b.op("vector", lambda h: h.scalar_tensor_tensor(
            out=arw[:], in0=as1[:], scalar=c.sb["ccols"][0:8, C_HALF:C_HALF + 1],
            in1=as2[:], op0=ALU.mult, op1=ALU.add), inc="vv")
        vv = b.op("vector", lambda h: h.tensor_scalar(
            out=arw[:], in0=arw[:], scalar1=nea[:], scalar2=None, op0=ALU.mult),
            inc="vv")
        sv = b.op("scalar", lambda h: h.activation(out=arw[:], in_=arw[:],
                                                   func=AF.Exp), inc="ss",
                  waits=[("vv", vv)])
        b.dma("gpsimd", io["dec_st"][0:8, t0:t0 + TC], arw[:], inc="dG",
              waits=[("ss", sv)])
        b.barrier()
    c.pop()
    stage6b(c)


def stage6b(c):
    b, io = c.b, c.io
    c.push()
    bd4c2 = c.sbuf("ebd42", [4, P], F32)
    bo64c = c.sbuf("ebo64", [P, 2], F32)
    rep2c = c.sbuf("erep2", [2, P], F32)
    b.dma("sync", bd4c2[:], io["c_bd4T"][:])
    b.dma("sync", bo64c[:], io["c_bo64"][:])
    b.dma("sync", rep2c[:], io["c_rep2"][:])
    wot = c.sbuf("ewo", [P, 4 * D], BF16)
    estg2 = c.sbuf("estg2", [P, 2048], F32)
    wstream(c, io["eb_WoT"], 4, D, wot, estg2)
    onwc = c.sbuf("eonw", [P, 4], F32)
    for tl in range(4):
        b.dma("sync", onwc[:, tl:tl + 1], io["eb_onw"][tl * P:(tl + 1) * P, 0:1])
    knbd = [c.sbuf("eknbd0", [P, 4 * TC], F32), c.sbuf("eknbd1", [P, 4 * TC], F32)]
    qnbd = [c.sbuf("eqnbd0", [P, 4 * TC], F32), c.sbuf("eqnbd1", [P, 4 * TC], F32)]
    knc = [c.sbuf("eknc0", [P, TC], F32), c.sbuf("eknc1", [P, TC], F32)]
    dsc = [c.sbuf("edsc0", [P, TC], F32), c.sbuf("edsc1", [P, TC], F32)]
    drw = c.sbuf("edrw", [4, TC], F32)
    v4 = [c.sbuf("ev40", [4, 128 * 64], BF16), c.sbuf("ev41", [4, 128 * 64], BF16)]
    ofm4 = [c.sbuf("eofm40", [64, 4 * TC], F32), c.sbuf("eofm41", [64, 4 * TC], F32)]
    Ssb = c.sbuf("eS", [P, 128], F32)
    tmpk = [c.sbuf("etmpk0", [P, 64], F32), c.sbuf("etmpk1", [P, 64], F32)]
    d4sb = [c.sbuf("ed40", [4, 64], F32), c.sbuf("ed41", [4, 64], F32)]
    ofm_f = c.sbuf("eofmf", [P, 4 * TC], F32)
    obf = c.sbuf("eobf", [P, 4 * TC], BF16)
    gtc = c.sbuf("egtc", [P, 4 * TC], BF16)
    xc7 = c.sbuf("exc7", [P, 8 * TC], F32)
    rr2 = c.sbuf("err2", [2, TC], F32)
    t1p = c.sbuf("et1", [P, TC], F32)
    t2p = c.sbuf("et2", [P, TC], F32)
    b.op("vector", lambda h: h.memset(Ssb[:], 0.0), inc="vv")
    for i in (0, 1):
        b.op("vector", lambda h, i=i: h.memset(knbd[i][:], 0.0), inc="vv")
        b.op("vector", lambda h, i=i: h.memset(qnbd[i][:], 0.0), inc="vv")
    b.barrier()
    for tc_i in range(NTC):
        t0 = tc_i * TC
        for i in (0, 1):
            for hh in range(4):
                hg = i * 4 + hh
                b.dma("sync",
                      knbd[i][hh * 32:(hh + 1) * 32, hh * TC:(hh + 1) * TC],
                      io["kn_st"][hg * 32:(hg + 1) * 32, t0:t0 + TC])
                b.dma("sync",
                      qnbd[i][hh * 32:(hh + 1) * 32, hh * TC:(hh + 1) * TC],
                      io["qn_st"][hg * 32:(hg + 1) * 32, t0:t0 + TC])
            b.dma("sync", knc[i][:], io["kn_st"][i * P:(i + 1) * P, t0:t0 + TC])
            w_ = [("mm", mmd)] if i == 1 else []
            b.dma("sync", drw[0:4, 0:TC], io["dec_st"][i * 4:(i + 1) * 4, t0:t0 + TC],
                  waits=w_)
            dvd = b.val("dL")
            i4, cond4 = c.bank()
            mmd = b.op("tensor", lambda h, i4=i4: h.matmul(
                out=c.ps[i4][:, 0:TC], lhsT=bd4c2[:], rhs=drw[0:4, 0:TC],
                start=True, stop=True), inc="mm",
                waits=[("dL", dvd)] + c.wc(cond4))
            vvd = b.op("vector", lambda h, i4=i4, i=i: h.tensor_copy(
                out=dsc[i][:], in_=c.ps[i4][:, 0:TC]), inc="vv",
                waits=[("mm", mmd)])
            c.bank_release(i4, "vv", vvd)
        b.barrier()
        tanh_val = b.val("ss")
        sqmm = [None, None]
        subvv = [None, None]
        for sb_i in range(4):
            for i in (0, 1):
                w_ = [("vv", subvv[i])] if subvv[i] is not None else []
                b.dma("sync",
                      v4[i].rearrange("p (v t) -> p v t", t=128)[0:4, :, :],
                      io["v_st"][i * 256:(i + 1) * 256,
                                 t0 + sb_i * 128:t0 + (sb_i + 1) * 128].rearrange(
                          "(h v) t -> h v t", v=64), waits=w_)
            dv4 = b.val("dL")
            for tt in range(128):
                t_loc = sb_i * 128 + tt
                stt_vals = [None, None]
                for i in (0, 1):
                    ia, ca = c.bank()
                    mm1 = b.op("tensor", lambda h, ia=ia, i=i, t_loc=t_loc: h.matmul(
                        out=c.ps[ia][0:4, 0:64],
                        lhsT=knbd[i].rearrange("p (f t) -> p f t", t=TC)[:, :, t_loc],
                        rhs=Ssb[:, i * 64:(i + 1) * 64], start=True, stop=True),
                        inc="mm", waits=[("ss", tanh_val)] + c.wc(ca))
                    w_ = [("mm", mm1)]
                    if tt == 0:
                        w_.append(("dL", dv4))
                    subvv[i] = b.op("vector", lambda h, ia=ia, i=i, tt=tt:
                                    h.tensor_sub(
                        out=d4sb[i][0:4, 0:64],
                        in0=v4[i].rearrange("p (v t) -> p v t", t=128)[0:4, :, tt],
                        in1=c.ps[ia][0:4, 0:64]), inc="vv", waits=w_, chain=True)
                    c.bank_release(ia, "vv", subvv[i])
                    ib, cb = c.bank()
                    mm2 = b.op("tensor", lambda h, ib=ib, i=i: h.matmul(
                        out=c.ps[ib][:, 0:64], lhsT=bd4c2[:], rhs=d4sb[i][0:4, 0:64],
                        start=True, stop=True), inc="mm",
                        waits=[("vv", subvv[i])] + c.wc(cb))
                    vv2 = b.op("vector", lambda h, ib=ib, i=i, t_loc=t_loc:
                               h.tensor_scalar(
                        out=tmpk[i][:], in0=c.ps[ib][:, 0:64],
                        scalar1=knc[i][:, t_loc:t_loc + 1], scalar2=None,
                        op0=ALU.mult), inc="vv", waits=[("mm", mm2)], chain=True)
                    c.bank_release(ib, "vv", vv2)
                    w_ = [("mm", sqmm[i])] if sqmm[i] is not None else []
                    stt_vals[i] = b.op("vector", lambda h, i=i, t_loc=t_loc:
                                       h.scalar_tensor_tensor(
                        out=Ssb[:, i * 64:(i + 1) * 64],
                        in0=Ssb[:, i * 64:(i + 1) * 64],
                        scalar=dsc[i][:, t_loc:t_loc + 1], in1=tmpk[i][:],
                        op0=ALU.mult, op1=ALU.add), inc="vv", waits=w_, chain=True)
                tanh_val = b.op("scalar", lambda h: h.activation(
                    out=Ssb[:], in_=Ssb[:], func=AF.Tanh), inc="ss",
                    waits=[("vv", stt_vals[1])], chain=True)
                for i in (0, 1):
                    ic, cc_ = c.bank()
                    sqmm[i] = b.op("tensor", lambda h, ic=ic, i=i, t_loc=t_loc:
                                   h.matmul(
                        out=c.ps[ic][0:64, 0:4], lhsT=Ssb[:, i * 64:(i + 1) * 64],
                        rhs=qnbd[i].rearrange("p (f t) -> p f t", t=TC)[:, :, t_loc],
                        start=True, stop=True), inc="mm",
                        waits=[("ss", tanh_val)] + c.wc(cc_))
                    vv4 = b.op("vector", lambda h, ic=ic, i=i, t_loc=t_loc:
                               h.tensor_copy(
                        out=ofm4[i][0:64, t_loc * 4:(t_loc + 1) * 4],
                        in_=c.ps[ic][0:64, 0:4]), inc="vv",
                        waits=[("mm", sqmm[i])], chain=True)
                    c.bank_release(ic, "vv", vv4)
        for i in (0, 1):
            for hh in range(4):
                hg = i * 4 + hh
                b.op("vector", lambda h, i=i, hh=hh, hg=hg: h.tensor_copy(
                    out=ofm_f[(hg % 2) * 64:(hg % 2) * 64 + 64,
                              (hg // 2) * TC:(hg // 2 + 1) * TC],
                    in_=ofm4[i].rearrange("p (t f) -> p t f", f=4)[0:64, :, hh]),
                    inc="vv")
        for k in range(4):
            b.dma("sync", gtc[:, k * TC:(k + 1) * TC],
                  io["gt_st"][k * P:(k + 1) * P, t0:t0 + TC])
        for k in range(8):
            b.dma("sync", xc7[:, k * TC:(k + 1) * TC],
                  io["xres"][k * P:(k + 1) * P, t0:t0 + TC])
        dvp = b.val("dL")
        ovv = b.val("vv")
        for j in range(4):
            osl = ofm_f[:, j * TC:(j + 1) * TC]
            b.op("scalar", lambda h, osl=osl: h.activation(
                out=t1p[:], in_=osl, func=AF.Square), inc="ss",
                waits=[("vv", ovv)])
            sv = b.op("scalar", lambda h, osl=osl: h.activation(
                out=t2p[:], in_=osl, func=AF.Sigmoid), inc="ss")
            b.op("vector", lambda h, osl=osl: h.tensor_mul(
                out=osl, in0=t1p[:], in1=t2p[:]), inc="vv", waits=[("ss", sv)])
            vvo = b.op("vector", lambda h, osl=osl: h.tensor_mul(
                out=t1p[:], in0=osl, in1=osl), inc="vv")
            i5, cond5 = c.bank()
            mmr = b.op("tensor", lambda h, i5=i5: h.matmul(
                out=c.ps[i5][0:2, 0:TC], lhsT=bo64c[:], rhs=t1p[:],
                start=True, stop=True), inc="mm", waits=[("vv", vvo)] + c.wc(cond5))
            vvr = b.op("vector", lambda h, i5=i5: h.tensor_copy(
                out=rr2[0:2, 0:TC], in_=c.ps[i5][0:2, 0:TC]), inc="vv",
                waits=[("mm", mmr)])
            c.bank_release(i5, "vv", vvr)
            b.op("scalar", lambda h: h.activation(
                out=rr2[0:2, 0:TC], in_=rr2[0:2, 0:TC], func=AF.Ln,
                scale=c.sb["ccols"][0:2, C_INV64:C_INV64 + 1],
                bias=c.sb["ccols"][0:2, C_EPS5:C_EPS5 + 1]), inc="ss",
                waits=[("vv", vvr)])
            svr = b.op("scalar", lambda h: h.activation(
                out=rr2[0:2, 0:TC], in_=rr2[0:2, 0:TC], func=AF.Exp,
                scale=c.sb["ccols"][0:2, C_NEGH:C_NEGH + 1]), inc="ss")
            i6, cond6 = c.bank()
            mmb = b.op("tensor", lambda h, i6=i6: h.matmul(
                out=c.ps[i6][:, 0:TC], lhsT=rep2c[:], rhs=rr2[0:2, 0:TC],
                start=True, stop=True), inc="mm", waits=[("ss", svr)] + c.wc(cond6))
            b.op("vector", lambda h, i6=i6, osl=osl: h.tensor_mul(
                out=osl, in0=osl, in1=c.ps[i6][:, 0:TC]), inc="vv",
                waits=[("mm", mmb)])
            vvz = b.op("vector", lambda h, osl=osl, j=j: h.tensor_scalar(
                out=osl, in0=osl, scalar1=onwc[:, j:j + 1], scalar2=None,
                op0=ALU.mult), inc="vv")
            c.bank_release(i6, "vv", vvz)
            ovv = b.op("vector", lambda h, osl=osl, j=j: h.tensor_mul(
                out=obf[:, j * TC:(j + 1) * TC], in0=osl,
                in1=gtc[:, j * TC:(j + 1) * TC]), inc="vv",
                waits=[("dL", dvp)])
        b.op("tensor", lambda h: None, waits=[("vv", ovv)])
        for ot in range(8):
            i, cond = c.bank()
            pt = c.ps[i]
            for dk in range(4):
                b.op("tensor", lambda h, dk=dk, ot=ot, pt=pt: h.matmul(
                    out=pt[:, 0:TC],
                    lhsT=wot[:, dk * D + ot * P: dk * D + (ot + 1) * P],
                    rhs=obf[:, dk * TC:(dk + 1) * TC], start=(dk == 0), stop=(dk == 3)),
                    inc="mm" if dk == 3 else None, waits=c.wc(cond) if dk == 0 else ())
            mmv = b.val("mm")
            xck = xc7[:, ot * TC:(ot + 1) * TC]
            vv = b.op("vector", lambda h, xck=xck, pt=pt: h.tensor_add(
                out=xck, in0=xck, in1=pt[:, 0:TC]), inc="vv", waits=[("mm", mmv)])
            c.bank_release(i, "vv", vv)
            b.dma("gpsimd", io["xres"][ot * P:(ot + 1) * P, t0:t0 + TC], xck,
                  inc="dG", waits=[("vv", vv)])
        b.barrier()
    c.pop()


def stage2_call(c):
    b, io = c.b, c.io
    ln_driver_sweep(c, io["xres"], io["xi_st"])
    for l in range(2):
        if STOP <= 8 + l * 2:
            return
        s2_attn(c, l)
        if STOP <= 9 + l * 2:
            return
        s2_ffn(c, l)
    add_sweep(c, io["xres"], io["xi_st"])


def dump_hbm(c, src_hbm, ntiles):
    b = c.b
    for k in range(ntiles):
        for tc_i in range(NTC):
            t0 = tc_i * TC
            dv = b.dma("sync", c.sb["dbgtmp"][:, 0:TC],
                       src_hbm[k * P:(k + 1) * P, t0:t0 + TC],
                       waits=[("dG", b.val("dG"))])
            b.dma("gpsimd", c.io["dbg"][k * P:(k + 1) * P, t0:t0 + TC],
                  c.sb["dbgtmp"][:, 0:TC], inc="dG", waits=[("dL", dv)])
    b.barrier()


def build_program(dbg=None):
    nc = bass.Bass("TRN2", target_bir_lowering=False, detect_race_conditions=False)
    dbg_shapes = {"xn1": (D, T), "xnd": (512, T), "ydss": (512, T), "x1": (D, T),
                  "x2": (D, T), "x2i": (D, T), "x3": (D, T), "x4": (D, T),
                  "x5": (D, T), "x6": (D, T), "x7": (D, T),
                  "e88kn": (256, T), "e88qn": (256, T), "e88dec": (8, T),
                  "rows1": (8, T), "rbc": (P, T), "xmid": (D, T),
                  "dsschk": (640, 1024), "lamchk": (640, 1024)}
    c = Ctx(nc, dbg=dbg)
    alloc_io(c, dbg_shapes.get(dbg))
    c.psum_init()

    c.sbuf("ccols", [P, N_CCOLS], F32)
    c.sbuf("iota", [P, KC + 1], F32)
    c.sbuf("riota", [P, KC], F32)
    c.sbuf("ones_bf", [P, 1], BF16)
    c.sbuf("ones_f", [P, 1], F32)
    c.sbuf("ones_row", [1, TC], F32)
    for rn in ("row_s0", "row_s1", "row_m", "row_sc", "rowr", "rowmr"):
        c.sbuf(rn, [1, TC], F32)
    c.sbuf("mski", [P, 4], mybir.dt.int32)
    c.sbuf("mskf", [P, 4], F32)
    c.sbuf("eye2", [P, P], F32)
    c.sbuf("dbgtmp", [P, TC], F32)

    blk = nc.Block()
    block = blk.__enter__()
    b = c.b
    b.dma("sync", c.sb["ccols"][:], c.io["c_cols"][:])
    b.dma("sync", c.sb["iota"][:], c.io["c_iota"][:])
    b.dma("sync", c.sb["riota"][:], c.io["c_riota"][:])
    b.dma("sync", c.sb["ones_f"][:], c.io["c_ones_col"][:])
    b.dma("sync", c.sb["ones_row"][:], c.io["c_ones_row"][:])
    b.dma("sync", c.sb["eye2"][:], c.io["c_eye"][:])
    b.op("vector", lambda h: h.tensor_copy(out=c.sb["ones_bf"][:], in_=c.sb["ones_f"][:]),
         inc="vv", waits=[("dL", b.val("dL"))])
    b.barrier()

    if START < 8:
        c.push()
        stage1(c)
        c.pop()
    if STOP > 7 and START < 12:
        c.push()
        stage2_call(c)
        c.pop()
        if c.dbg == "x2":
            dump_hbm(c, c.io["xres"], 8)
        if c.dbg == "x2i":
            dump_hbm(c, c.io["xi_st"], 8)
    if STOP > 12 and START < 14:
        c.push()
        stage3(c)
        c.pop()
        if c.dbg == "x3":
            dump_hbm(c, c.io["xres"], 8)
    if STOP > 13 and START < 15:
        c.push()
        stage2_call(c)
        c.pop()
        if c.dbg == "x4":
            dump_hbm(c, c.io["xres"], 8)
    if STOP > 14 and START < 16:
        c.push()
        stage4(c)
        c.pop()
        if c.dbg == "x5":
            dump_hbm(c, c.io["xres"], 8)
    if STOP > 15 and START < 17:
        c.push()
        stage5(c, first=True)
        c.pop()
        if c.dbg == "x6":
            dump_hbm(c, c.io["xres"], 8)
    if STOP > 16 and START < 18:
        c.push()
        stage6(c)
        c.pop()
        if c.dbg == "x7":
            dump_hbm(c, c.io["xres"], 8)
        if c.dbg == "e88kn":
            dump_hbm(c, c.io["kn_st"], 2)
        if c.dbg == "e88qn":
            dump_hbm(c, c.io["qn_st"], 2)
        if c.dbg == "e88dec":
            for tc_i in range(NTC):
                t0 = tc_i * TC
                dv = c.b.dma("sync", c.sb["dbgtmp"][0:8, 0:TC],
                             c.io["dec_st"][0:8, t0:t0 + TC],
                             waits=[("dG", c.b.val("dG"))])
                c.b.dma("gpsimd", c.io["dbg"][0:8, t0:t0 + TC],
                        c.sb["dbgtmp"][0:8, 0:TC], inc="dG", waits=[("dL", dv)])
            c.b.barrier()
    if STOP > 17 and START < 19:
        c.push()
        stage5(c, first=not (STOP > 15 and START < 17))
        c.pop()

    if c.dbg == "x1":
        xfin = c.sbuf("xfin", [P, TC], F32)
        for k in range(8):
            for tc_i in range(NTC):
                t0 = tc_i * TC
                dv = b.dma("sync", xfin[:], c.io["xres"][k * P:(k + 1) * P, t0:t0 + TC],
                           waits=[("dG", b.val("dG"))])
                b.dma("gpsimd", c.io["dbg"][k * P:(k + 1) * P, t0:t0 + TC], xfin[:],
                      inc="dG", waits=[("dL", dv)])
        b.barrier()
    xout_src = c.io["xres"] if STOP > 6 else c.io["x_in"]
    xc = c.sbuf("xout", [P, TC], F32)
    for k in range(8):
        for tc_i in range(NTC):
            t0 = tc_i * TC
            dv = b.dma("sync", xc[:], xout_src[k * P:(k + 1) * P, t0:t0 + TC],
                       waits=[("dG", b.val("dG"))])
            b.dma("gpsimd", c.io["out"][k * P:(k + 1) * P, t0:t0 + TC], xc[:],
                  inc="dG", waits=[("dL", dv)])
    b.barrier()

    b.finalize(block)
    blk.__exit__(None, None, None)
    return nc, c


def kernel(**inputs):
    dev, per_core = host_prep(inputs)
    nc, _ = build_program()
    in_maps = []
    for pc in per_core:
        m = dict(dev)
        m.update(pc)
        in_maps.append(m)
    res = run_bass_kernel_spmd(nc, in_maps, core_ids=list(range(8)))
    out = np.empty((4, T, D), dtype=np.float32)
    for ci in range(8):
        bidx, half = ci // 2, ci % 2
        o = res.results[ci]["out"]
        sl = slice(half * (T // 2), (half + 1) * (T // 2))
        out[bidx, sl, :] = o[:, sl].T
    return out
